# version 03 (final)
# baseline (speedup 1.0000x reference)
import numpy as np
import ml_dtypes
from contextlib import ExitStack
import concourse.bass as bass
import concourse.mybir as mybir
from concourse.bass_utils import run_bass_kernel_spmd

F32 = mybir.dt.float32
BF16 = mybir.dt.bfloat16
I32 = mybir.dt.int32
U8 = mybir.dt.uint8
AF = mybir.ActivationFunctionType
ALU = mybir.AluOpType

NCORES = 8
S = 2048
D = 1024
MAGIC_RN = 12582912.0
TT = 512
NTT = S // TT
NCH = D // 128

ENGINES = ("sync", "gpsimd", "vector", "scalar", "tensor")
DMA_ENGINES = ("sync", "gpsimd", "scalar")


class _TensorProxy:

    def __init__(self, e):
        self._e = e

    def __getattr__(self, name):
        return getattr(self._e, name)

    def matmul(self, *a, **kw):
        kw.setdefault("skip_group_check", True)
        return self._e.matmul(*a, **kw)


class Prog:

    def __init__(self, nc):
        self.nc = nc
        self.es = ExitStack()
        self.prefix = ""
        self._scopes = []
        self.items = {e: [] for e in ENGINES}
        self.opsem = {e: self.es.enter_context(nc.semaphore(f"op_{e}"))
                      for e in ENGINES}
        self.NSLOT = 24
        self.dmaslots = {e: [self.es.enter_context(nc.semaphore(f"dq_{e}{i}"))
                             for i in range(self.NSLOT)] for e in DMA_ENGINES}
        self.dmagen = {e: [0] * self.NSLOT for e in DMA_ENGINES}
        self.dmaidx = {e: 0 for e in DMA_ENGINES}
        self.opcnt = {e: 0 for e in ENGINES}
        self.last_writer = {}
        self.last_writer_eng = {}
        self.readers = {}
        self.waited = {e: {} for e in ENGINES}
        self._names = set()

    def _es(self):
        return self._scopes[-1] if self._scopes else self.es

    def sbuf(self, name, shape, dt):
        name = self.prefix + name
        assert name not in self._names, name
        self._names.add(name)
        return self._es().enter_context(
            self.nc.sbuf_tensor(name, list(shape), dt))

    def psum(self, name, shape=(128, 512), dt=F32):
        name = self.prefix + name
        assert name not in self._names, name
        self._names.add(name)
        return self._es().enter_context(
            self.nc.psum_tensor(name, list(shape), dt))

    def dram(self, name, shape, dt):
        name = self.prefix + name
        return self.nc.dram_tensor(name, list(shape), dt, kind="Internal").ap()

    def scope_open(self, tag):
        self.prefix = tag + "_"
        self._scopes.append(ExitStack())

    def scope_close(self):
        self.full_barrier()
        self._scopes.pop().close()
        self.prefix = ""

    def full_barrier(self):
        deps = []
        for e in ENGINES:
            if self.opcnt[e] > 0:
                deps.append((self.opsem[e], self.opcnt[e]))
        for e in DMA_ENGINES:
            for i in range(self.NSLOT):
                g = self.dmagen[e][i]
                if g > 0:
                    deps.append((self.dmaslots[e][i], 16 * g))
        for d in deps:
            self._wait("gpsimd", d)
        self.items["gpsimd"].append([lambda g: g.nop(),
                                     [(self.opsem["gpsimd"], 1)]])
        self.opcnt["gpsimd"] += 1
        bar = (self.opsem["gpsimd"], self.opcnt["gpsimd"])
        for e in ENGINES:
            if e != "gpsimd":
                self._wait(e, bar)
        self.last_writer = {}
        self.last_writer_eng = {}
        self.readers = {}

    def _wait(self, eng, dep):
        if dep is None:
            return
        sem, val = dep
        w = self.waited[eng]
        if w.get(sem, -1) >= val:
            return
        w[sem] = val
        self.items[eng].append([("wait", sem, val), None])

    def op(self, eng, fn, reads=(), writes=(), dma=False):
        for r in reads:
            for d in self.last_writer.get(id(r), ()):
                self._wait(eng, d)
        for wr in writes:
            if dma:
                for (reng, rdep) in self.readers.get(id(wr), ()):
                    self._wait(eng, rdep)
            else:
                for d in self.last_writer.get(id(wr), ()):
                    self._wait(eng, d)
                for (reng, rdep) in self.readers.get(id(wr), ()):
                    self._wait(eng, rdep)
        if dma:
            i = self.dmaidx[eng] % self.NSLOT
            self.dmaidx[eng] += 1
            sem = self.dmaslots[eng][i]
            g = self.dmagen[eng][i]
            if g > 0:
                self._wait(eng, (sem, 16 * g))
            self.dmagen[eng][i] = g + 1
            dep = (sem, 16 * (g + 1))
            inc = 16
        else:
            sem = self.opsem[eng]
            self.opcnt[eng] += 1
            dep = (sem, self.opcnt[eng])
            inc = 1
        self.items[eng].append([fn, [(sem, inc)]])
        for r in reads:
            self.readers.setdefault(id(r), []).append((eng, dep))
        for wr in writes:
            if dma:
                self.last_writer.setdefault(id(wr), []).append(dep)
            else:
                self.last_writer[id(wr)] = [dep]
            self.last_writer_eng[id(wr)] = eng
            self.readers[id(wr)] = []
        return dep

    def dma(self, eng, out, in_, reads=(), writes=(), **kw):
        return self.op(eng, lambda e: e.dma_start(out=out, in_=in_, **kw),
                       reads=reads, writes=writes, dma=True)

    def barrier_dep(self, eng):
        if self.opcnt[eng] == 0:
            return None
        return (self.opsem[eng], self.opcnt[eng])

    def build(self, final_deps):
        for d in final_deps:
            self._wait("gpsimd", d)
        nc = self.nc
        with nc.Block() as block:
            for eng in ENGINES:
                items = self.items[eng]
                if not items:
                    continue

                def body(e, items=items, eng=eng):
                    if eng == "tensor":
                        e = _TensorProxy(e)
                    for it in items:
                        fn, incs = it[0], it[1]
                        if isinstance(fn, tuple):
                            e.wait_ge(fn[1], fn[2])
                        else:
                            ins = fn(e)
                            for (sem, n) in incs:
                                ins.then_inc(sem, n)
                getattr(block, eng)(body)
        self.es.close()
        return nc



def emit_consts(p):
    c = {}
    c["ones_col_bf"] = p.sbuf("c_ones_col_bf", [128, 1], BF16)
    c["ones_row_bf"] = p.sbuf("c_ones_row_bf", [1, 128], BF16)
    c["ones_full_f"] = p.sbuf("c_ones_full_f", [128, 128], F32)
    c["rowid"] = p.sbuf("c_rowid", [128, 1], F32)
    c["colramp"] = p.sbuf("c_colramp", [128, 128], F32)
    c["ident_f"] = p.sbuf("c_ident_f", [128, 128], F32)
    c["ident_bf"] = p.sbuf("c_ident_bf", [128, 128], BF16)
    c["halfpi"] = p.sbuf("c_halfpi", [128, 1], F32)
    c["mask_ut"] = p.sbuf("c_mask_ut", [128, 128], BF16)
    c["ones_row_bf4"] = p.sbuf("c_ones_row_bf4", [1, 512], BF16)
    scratch = p.dram("c_scratch", [128], F32)

    p.op("vector", lambda v: v.memset(c["ones_col_bf"][:], 1.0),
         writes=[c["ones_col_bf"]])
    p.op("vector", lambda v: v.memset(c["ones_row_bf"][:], 1.0),
         writes=[c["ones_row_bf"]])
    p.op("vector", lambda v: v.memset(c["ones_full_f"][:], 1.0),
         writes=[c["ones_full_f"]])
    p.op("vector", lambda v: v.memset(c["halfpi"][:], float(np.pi / 2)),
         writes=[c["halfpi"]])
    p.op("vector", lambda v: v.tensor_tensor_scan(
        c["colramp"][:], c["ones_full_f"][:], c["ones_full_f"][:], 0.0,
        op0=ALU.mult, op1=ALU.add), reads=[c["ones_full_f"]],
        writes=[c["colramp"]])
    p.op("vector", lambda v: v.tensor_scalar_add(
        c["colramp"][:], c["colramp"][:], -1.0), reads=[c["colramp"]],
        writes=[c["colramp"]])
    p.dma("sync", scratch.rearrange("(o x) -> o x", o=1), c["colramp"][0:1, :],
          reads=[c["colramp"]], writes=[scratch])
    p.dma("sync", c["rowid"][:], scratch.rearrange("(p x) -> p x", x=1),
          reads=[scratch], writes=[c["rowid"]])
    p.op("vector", lambda v: v.tensor_tensor(
        c["ident_f"][:], c["colramp"][:], c["rowid"][:].to_broadcast([128, 128]),
        op=ALU.is_equal), reads=[c["colramp"], c["rowid"]], writes=[c["ident_f"]])
    p.op("vector", lambda v: v.tensor_copy(c["ident_bf"][:], c["ident_f"][:]),
         reads=[c["ident_f"]], writes=[c["ident_bf"]])
    p.op("vector", lambda v: v.tensor_tensor(
        c["mask_ut"][:], c["colramp"][:], c["rowid"][:].to_broadcast([128, 128]),
        op=ALU.is_ge), reads=[c["colramp"], c["rowid"]], writes=[c["mask_ut"]])
    p.op("vector", lambda v: v.memset(c["ones_row_bf4"][:], 1.0),
         writes=[c["ones_row_bf4"]])
    return c


def emit_vec_to_col(p, c, name, hbm_vec, n, ps):
    ncol = n // 128
    tmp = p.sbuf(f"{name}_cm", [ncol, 128], F32)
    out = p.sbuf(name, [128, ncol], F32)
    p.dma("sync", tmp[:], hbm_vec.rearrange("(c p) -> c p", p=128),
          writes=[tmp])
    p.op("tensor", lambda t: t.transpose(ps[:, 0:ncol], tmp[:],
                                         c["ident_f"][0:ncol, 0:ncol]),
         reads=[tmp, c["ident_f"]], writes=[ps])
    p.op("vector", lambda v: v.tensor_copy(out[:], ps[:, 0:ncol]),
         reads=[ps], writes=[out])
    return out



def emit_weight_load(p, name, hbm_ap, rows, cols, colsum=False, c=None,
                     dt=BF16, psum_pool=None):
    nch = (rows + 127) // 128
    w = {"nch": nch, "tiles": [], "name": name, "rows": rows, "cols": cols}
    for k in range(nch):
        r0, r1 = k * 128, min(rows, (k + 1) * 128)
        t = p.sbuf(f"{name}_k{k}", [r1 - r0, cols], dt)
        w["tiles"].append(t)
        p.dma("gpsimd", t[:], hbm_ap[r0:r1, :], writes=[t])
    if colsum:
        pb = psum_pool
        w["csw"] = p.sbuf(f"{name}_csw", [1, cols], BF16)
        nslice = (cols + 511) // 512
        for s_ in range(nslice):
            c0, c1 = s_ * 512, min(cols, (s_ + 1) * 512)
            for k in range(nch):
                p.op("tensor", lambda t, k=k, c0=c0, c1=c1: t.matmul(
                    pb[0:1, 0:c1 - c0], c["ones_col_bf"][0:w["tiles"][k].shape[0], :],
                    w["tiles"][k][:, c0:c1], start=(k == 0), stop=(k == nch - 1)),
                    reads=[w["tiles"][k], c["ones_col_bf"]], writes=[pb])
            p.op("vector", lambda v, c0=c0, c1=c1: v.tensor_copy(
                w["csw"][0:1, c0:c1], pb[0:1, 0:c1 - c0]),
                reads=[pb], writes=[w["csw"]])
    return w


def alloc_ln_bufs(p, tag="", tt_len=TT, xs_width=None, xb=None):
    w = xs_width or tt_len
    return {
        "xb": xb if xb is not None else
              [p.sbuf(f"ln{tag}_xb{k}", [128, tt_len], BF16) for k in range(NCH)],
        "xs": [p.sbuf(f"ln{tag}_xs{k}", [128, w], BF16) for k in range(NCH)],
        "rows": p.sbuf(f"ln{tag}_rows", [1, 4 * tt_len], F32),
        "negrm": p.sbuf(f"ln{tag}_negrm", [1, w], BF16),
        "r_bf": p.sbuf(f"ln{tag}_rbf", [1, w], BF16),
    }


def emit_ln(p, c, x_tiles, bufs, psA, psB, tt_len=TT, off=0,
            rrow_f=None, nmrow_f=None, skip_xs=False):
    xb = bufs["xb"]
    xs = bufs["xs"]
    rows = bufs["rows"]
    negrm = bufs["negrm"]
    r_bf = bufs["r_bf"]
    o0, o1 = off, off + tt_len

    for k in range(NCH):
        p.op("vector", lambda v, k=k: v.tensor_copy(
            xb[k][:, 0:tt_len], x_tiles[k][:]),
            reads=[x_tiles[k]], writes=[xb[k]])
    sq = bufs.get("sq")
    if sq is None:
        sq = xb
    for k in range(NCH):
        p.op("tensor", lambda t, k=k: t.matmul(
            psA[0:1, 0:tt_len], c["ones_col_bf"][:], xb[k][:, 0:tt_len],
            start=(k == 0), stop=(k == NCH - 1)),
            reads=[xb[k], c["ones_col_bf"]], writes=[psA])
    for k in range(NCH):
        p.op("scalar", lambda s, k=k: s.activation(
            xb[k][:, 0:tt_len], xb[k][:, 0:tt_len], AF.Square),
            reads=[xb[k]], writes=[xb[k]])
    for k in range(NCH):
        p.op("tensor", lambda t, k=k: t.matmul(
            psB[0:1, 0:tt_len], c["ones_col_bf"][:], xb[k][:, 0:tt_len],
            start=(k == 0), stop=(k == NCH - 1)),
            reads=[xb[k], c["ones_col_bf"]], writes=[psB])
    mean = rows[0:1, 0:tt_len]
    var = rows[0:1, tt_len:2 * tt_len]
    tmp = rows[0:1, 2 * tt_len:3 * tt_len]
    rstd = rows[0:1, 3 * tt_len:4 * tt_len]
    p.op("vector", lambda v: v.tensor_scalar_mul(mean, psA[0:1, 0:tt_len], 1.0 / D),
         reads=[psA], writes=[rows])
    p.op("vector", lambda v: v.tensor_scalar_mul(var, psB[0:1, 0:tt_len], 1.0 / D),
         reads=[psB], writes=[rows])
    p.op("vector", lambda v: v.tensor_mul(tmp, mean, mean), reads=[rows], writes=[rows])
    p.op("vector", lambda v: v.tensor_tensor(var, var, tmp, op=ALU.subtract),
         reads=[rows], writes=[rows])
    p.op("vector", lambda v: v.tensor_scalar_add(var, var, 1e-5),
         reads=[rows], writes=[rows])
    p.op("scalar", lambda s: s.activation(tmp, var, AF.Sqrt), reads=[rows],
         writes=[rows])
    p.op("vector", lambda v: v.reciprocal(rstd, tmp), reads=[rows], writes=[rows])
    p.op("vector", lambda v: v.tensor_copy(r_bf[0:1, o0:o1], rstd), reads=[rows],
         writes=[r_bf])
    if rrow_f is not None:
        p.op("vector", lambda v: v.tensor_copy(rrow_f[0:1, o0:o1], rstd),
             reads=[rows], writes=[rrow_f])
    p.op("vector", lambda v: v.tensor_mul(tmp, mean, rstd), reads=[rows],
         writes=[rows])
    p.op("vector", lambda v: v.tensor_scalar_mul(negrm[0:1, o0:o1], tmp, -1.0),
         reads=[rows], writes=[negrm])
    if nmrow_f is not None:
        p.op("vector", lambda v: v.tensor_scalar_mul(nmrow_f[0:1, o0:o1], tmp, -1.0),
             reads=[rows], writes=[nmrow_f])
    if skip_xs:
        return {"xs": xs, "negrm": negrm, "rows": rows}
    p.op("tensor", lambda t: t.matmul(psA[:, 0:tt_len], c["ones_row_bf"][:],
                                      r_bf[0:1, o0:o1], start=True, stop=True),
         reads=[r_bf, c["ones_row_bf"]], writes=[psA])
    for k in range(NCH):
        p.op("vector", lambda v, k=k: v.tensor_mul(
            xs[k][:, o0:o1], x_tiles[k][:], psA[:, 0:tt_len]),
            reads=[x_tiles[k], psA], writes=[xs[k]])
    return {"xs": xs, "negrm": negrm, "rows": rows}


def emit_linear_acc(p, w, xs, negrm, psum_tile, c0, c1, tt_len=TT, rank1=True,
                    start=True, stop=True, xs_sl=None, nm_sl=None):
    n = w["nch"]
    for k in range(n):
        xop = xs[k][:] if xs_sl is None else xs[k][:, xs_sl[0]:xs_sl[1]]
        p.op("tensor", lambda t, k=k, xop=xop: t.matmul(
            psum_tile[0:c1 - c0, 0:tt_len], w["tiles"][k][:, c0:c1], xop,
            start=(start and k == 0), stop=(k == n - 1 and stop and not rank1)),
            reads=[w["tiles"][k], xs[k]], writes=[psum_tile])
    if rank1:
        nm = negrm[0:1, :] if nm_sl is None else negrm[0:1, nm_sl[0]:nm_sl[1]]
        p.op("tensor", lambda t, nm=nm: t.matmul(
            psum_tile[0:c1 - c0, 0:tt_len], w["csw"][0:1, c0:c1], nm,
            start=False, stop=stop),
            reads=[w["csw"], negrm], writes=[psum_tile])



def emit_l1_ffn(p, c, aps, TTL=256):
    x_ap = aps["x"]
    w1_ap = aps["W1"]
    w2_ap = aps["W2"]
    b1_ap = aps["b1"]
    b2_ap = aps["b2"]
    qp_ap = aps["qph"]
    qr_ap = aps["qwr"]
    qi_ap = aps["qwi"]
    y_ap = aps["y"]
    DFF = 4096
    NF = DFF // 128
    pool = [p.psum(f"pp{i}") for i in range(8)]
    w1 = emit_weight_load(p, "W1", w1_ap, D, DFF, colsum=True, c=c,
                          psum_pool=pool[6])
    w2 = emit_weight_load(p, "W2", w2_ap, DFF, D, colsum=False, c=c,
                          psum_pool=pool[7])

    b1t = emit_vec_to_col(p, c, "b1t", b1_ap, DFF, pool[0])
    b2t = emit_vec_to_col(p, c, "b2t", b2_ap, D, pool[1])
    tht = emit_vec_to_col(p, c, "tht", qp_ap, DFF, pool[2])
    qrt = emit_vec_to_col(p, c, "qrt", qr_ap, DFF, pool[3])
    qit = emit_vec_to_col(p, c, "qit", qi_ap, DFF, pool[0])

    scr = p.sbuf("qscr", [128, 2 * NF], F32)
    sinth = p.sbuf("sinth", [128, NF], F32)
    costh = p.sbuf("costh", [128, NF], F32)
    Acol = p.sbuf("Acol", [128, NF], F32)
    MAGIC = 12582912.0
    TWO_PI = float(2.0 * np.pi)
    u = scr[:, 0:NF]
    t2 = scr[:, NF:2 * NF]
    p.op("vector", lambda v: v.tensor_scalar(u, tht[:], 1.0 / TWO_PI, MAGIC,
                                             op0=ALU.mult, op1=ALU.add),
         reads=[tht], writes=[scr])
    p.op("vector", lambda v: v.tensor_scalar(u, u, -MAGIC, -TWO_PI,
                                             op0=ALU.add, op1=ALU.mult),
         reads=[scr], writes=[scr])
    p.op("vector", lambda v: v.tensor_tensor(u, u, tht[:], op=ALU.add),
         reads=[scr, tht], writes=[scr])
    p.op("scalar", lambda s: s.activation(sinth[:], u, AF.Sin), reads=[scr],
         writes=[sinth])
    p.op("scalar", lambda s: s.activation(costh[:], u, AF.Sin, scale=0.5),
         reads=[scr], writes=[costh])
    p.op("scalar", lambda s: s.activation(costh[:], costh[:], AF.Square),
         reads=[costh], writes=[costh])
    p.op("vector", lambda v: v.tensor_scalar(costh[:], costh[:], -2.0, 1.0,
                                             op0=ALU.mult, op1=ALU.add),
         reads=[costh], writes=[costh])
    p.op("vector", lambda v: v.tensor_mul(t2, costh[:], qrt[:]),
         reads=[costh, qrt], writes=[scr])
    p.op("vector", lambda v: v.tensor_mul(u, sinth[:], qit[:]),
         reads=[sinth, qit], writes=[scr])
    p.op("vector", lambda v: v.tensor_tensor(t2, t2, u, op=ALU.add),
         reads=[scr], writes=[scr])
    p.op("vector", lambda v: v.tensor_tensor(t2, t2, qit[:], op=ALU.mult),
         reads=[scr, qit], writes=[scr])
    p.op("vector", lambda v: v.tensor_scalar_mul(Acol[:], t2, 0.1),
         reads=[scr], writes=[Acol])

    store_deps = []
    h_bf = [p.sbuf(f"hbf_{j}", [128, TTL], BF16) for j in range(NF)]
    hl = p.sbuf("hl", [128, TTL], F32)
    sa = p.sbuf("sa", [128, TTL], F32)
    ca = p.sbuf("ca", [128, TTL], F32)
    xt = [p.sbuf(f"xt_{k}", [128, TTL], F32) for k in range(NCH)]
    ln_bufs = alloc_ln_bufs(p, tt_len=TTL)
    for tt in range(S // TTL):
        for k in range(NCH):
            p.dma("sync", xt[k][:],
                  x_ap[k * 128:(k + 1) * 128, tt * TTL:(tt + 1) * TTL],
                  writes=[xt[k]])
        ln = emit_ln(p, c, xt, ln_bufs, pool[4], pool[5], tt_len=TTL)
        for j in range(NF):
            pb = pool[j % 4]
            emit_linear_acc(p, w1, ln["xs"], ln["negrm"], pb, j * 128,
                            (j + 1) * 128, tt_len=TTL)
            p.op("vector", lambda v, pb=pb, j=j: v.tensor_scalar(
                hl[:], pb[:, 0:TTL], b1t[:, j:j + 1], None, op0=ALU.add),
                reads=[pb, b1t], writes=[hl])
            p.op("scalar", lambda s: s.activation(sa[:], hl[:], AF.Sin,
                                                  scale=0.1),
                 reads=[hl], writes=[sa])
            p.op("scalar", lambda s: s.activation(ca[:], hl[:], AF.Sin,
                                                  scale=0.05),
                 reads=[hl], writes=[ca])
            p.op("scalar", lambda s: s.activation(ca[:], ca[:], AF.Square),
                 reads=[ca], writes=[ca])
            p.op("vector", lambda v: v.tensor_scalar(ca[:], ca[:], -2.0, 1.0,
                                                     op0=ALU.mult, op1=ALU.add),
                 reads=[ca], writes=[ca])
            p.op("vector", lambda v, j=j: v.tensor_scalar(
                sa[:], sa[:], costh[:, j:j + 1], None, op0=ALU.mult),
                reads=[sa, costh], writes=[sa])
            p.op("vector", lambda v, j=j: v.scalar_tensor_tensor(
                sa[:], ca[:], sinth[:, j:j + 1], sa[:], op0=ALU.mult,
                op1=ALU.add), reads=[ca, sinth, sa], writes=[sa])
            p.op("vector", lambda v, j=j: v.scalar_tensor_tensor(
                sa[:], sa[:], Acol[:, j:j + 1], hl[:], op0=ALU.mult,
                op1=ALU.add), reads=[sa, Acol, hl], writes=[sa])
            p.op("vector", lambda v, j=j: v.tensor_scalar_max(
                h_bf[j][:], sa[:], 0.0), reads=[sa], writes=[h_bf[j]])
        for o in range(NCH):
            pb = pool[4 + (o % 2)]
            for k in range(NF):
                p.op("tensor", lambda t, k=k, o=o, pb=pb: t.matmul(
                    pb[:, 0:TTL], w2["tiles"][k][:, o * 128:(o + 1) * 128],
                    h_bf[k][:], start=(k == 0), stop=(k == NF - 1)),
                    reads=[w2["tiles"][k], h_bf[k]], writes=[pb])
            p.op("vector", lambda v, pb=pb, o=o: v.tensor_scalar(
                pb[:, 0:TTL], pb[:, 0:TTL], b2t[:, o:o + 1], 0.5,
                op0=ALU.add, op1=ALU.mult), reads=[pb, b2t], writes=[pb])
            p.op("vector", lambda v, pb=pb, o=o: v.tensor_tensor(
                xt[o][:], xt[o][:], pb[:, 0:TTL], op=ALU.add),
                reads=[pb, xt[o]], writes=[xt[o]])
            store_deps.append(p.dma(
                "sync", y_ap[o * 128:(o + 1) * 128, tt * TTL:(tt + 1) * TTL],
                xt[o][:], reads=[xt[o]]))

    return store_deps[-NCH:]


def build_l1_ffn(TTL=256):
    nc = bass.Bass("TRN2", target_bir_lowering=False, debug=False)
    p = Prog(nc)
    DFF = 4096
    NF = DFF // 128
    x_ap = nc.dram_tensor("x", [D, S], F32, kind="ExternalInput").ap()
    w1_ap = nc.dram_tensor("W1", [D, DFF], F32, kind="ExternalInput").ap()
    w2_ap = nc.dram_tensor("W2", [DFF, D], F32, kind="ExternalInput").ap()
    b1_ap = nc.dram_tensor("b1", [DFF], F32, kind="ExternalInput").ap()
    b2_ap = nc.dram_tensor("b2", [D], F32, kind="ExternalInput").ap()
    qp_ap = nc.dram_tensor("qph", [DFF], F32, kind="ExternalInput").ap()
    qr_ap = nc.dram_tensor("qwr", [DFF], F32, kind="ExternalInput").ap()
    qi_ap = nc.dram_tensor("qwi", [DFF], F32, kind="ExternalInput").ap()
    y_ap = nc.dram_tensor("y", [D, S], F32, kind="ExternalOutput").ap()

    c = emit_consts(p)
    sd = emit_l1_ffn(p, c, {"x": x_ap, "W1": w1_ap, "W2": w2_ap, "b1": b1_ap, "b2": b2_ap, "qph": qp_ap, "qwr": qr_ap, "qwi": qi_ap, "y": y_ap}, TTL=TTL)
    return p.build(sd), ["x", "W1", "W2", "b1", "b2", "qph",
                                        "qwr", "qwi"]


def emit_l0_moe(p, c, aps, TTL=512):
    x_ap = aps["x"]
    wg_ap = aps["Wg"]
    ws1_ap = aps["Ws1"]
    ws3_ap = aps["Ws3"]
    ws2_ap = aps["Ws2"]
    we1_ap = aps["We1"]
    we3_ap = aps["We3"]
    we2_ap = aps["We2"]
    y_ap = aps["y"]
    FF = 2048
    NFJ = FF // 128
    HS = S // 2
    NT = HS // TTL
    pool = [p.psum(f"pp{i}") for i in range(8)]

    wg = [p.sbuf(f"wg_k{k}", [128, 8], F32) for k in range(NCH)]
    for k in range(NCH):
        p.dma("sync", wg[k][:], wg_ap[k * 128:(k + 1) * 128, :], writes=[wg[k]])
    cswg_row = p.sbuf("cswg_row", [1, 8], F32)
    cswg_nb = p.sbuf("cswg_nb", [128, 8], F32)
    for k in range(NCH):
        p.op("tensor", lambda t, k=k: t.matmul(
            pool[6][0:1, 0:8], c["ones_full_f"][:, 0:1], wg[k][:],
            start=(k == 0), stop=(k == NCH - 1)),
            reads=[wg[k], c["ones_full_f"]], writes=[pool[6]])
    p.op("vector", lambda v: v.tensor_copy(cswg_row[:], pool[6][0:1, 0:8]),
         reads=[pool[6]], writes=[cswg_row])
    p.op("tensor", lambda t: t.matmul(pool[6][:, 0:8], c["ones_full_f"][0:1, :],
                                      cswg_row[:], start=True, stop=True),
         reads=[cswg_row, c["ones_full_f"]], writes=[pool[6]])
    p.op("vector", lambda v: v.tensor_scalar_mul(cswg_nb[:], pool[6][:, 0:8], -1.0),
         reads=[pool[6]], writes=[cswg_nb])

    w1b = {"nch": NCH, "tiles": [p.sbuf(f"w1b_k{k}", [128, FF], BF16)
                                 for k in range(NCH)],
           "csw": p.sbuf("w1b_csw", [1, FF], BF16)}
    w3b = {"nch": NCH, "tiles": [p.sbuf(f"w3b_k{k}", [128, FF], BF16)
                                 for k in range(NCH)],
           "csw": p.sbuf("w3b_csw", [1, FF], BF16)}
    w2b = [p.sbuf(f"w2b_k{k}", [128, D], BF16) for k in range(NFJ)]

    def load_expertlike(ap1, ap3, ap2, e=None):
        for k in range(NCH):
            a1 = ap1[k * 128:(k + 1) * 128, :] if e is None else \
                ap1[e, k * 128:(k + 1) * 128, :]
            a3 = ap3[k * 128:(k + 1) * 128, :] if e is None else \
                ap3[e, k * 128:(k + 1) * 128, :]
            p.dma("gpsimd", w1b["tiles"][k][:], a1, writes=[w1b["tiles"][k]])
            p.dma("gpsimd", w3b["tiles"][k][:], a3, writes=[w3b["tiles"][k]])
        for k in range(NFJ):
            a2 = ap2[k * 128:(k + 1) * 128, :] if e is None else \
                ap2[e, k * 128:(k + 1) * 128, :]
            p.dma("gpsimd", w2b[k][:], a2, writes=[w2b[k]])
        for wd in (w1b, w3b):
            for s_ in range(FF // 512):
                c0, c1 = s_ * 512, (s_ + 1) * 512
                for k in range(NCH):
                    p.op("tensor", lambda t, wd=wd, k=k, c0=c0, c1=c1: t.matmul(
                        pool[7][0:1, 0:512], c["ones_col_bf"][:],
                        wd["tiles"][k][:, c0:c1], start=(k == 0),
                        stop=(k == NCH - 1)),
                        reads=[wd["tiles"][k], c["ones_col_bf"]],
                        writes=[pool[7]])
                p.op("vector", lambda v, wd=wd, c0=c0, c1=c1: v.tensor_copy(
                    wd["csw"][0:1, c0:c1], pool[7][0:1, 0:512]),
                    reads=[pool[7]], writes=[wd["csw"]])

    h_bf = [p.sbuf(f"hbf_{j}", [128, TTL], BF16) for j in range(NFJ)]
    ln_bufs = alloc_ln_bufs(p, tt_len=TTL, xs_width=HS, xb=h_bf[0:NCH])
    y_acc = [p.sbuf(f"yacc_{o}", [128, HS], F32) for o in range(NCH)]
    w_tm = p.sbuf("w_tm", [128, 8 * (HS // 128)], F32)
    w_row = p.sbuf("w_row", [1, HS], F32)
    xt = [p.sbuf(f"xt_{k}", [128, TTL], F32) for k in range(NCH)]
    gsc = p.sbuf("gsc", [128, 3 * 8], F32)
    gmax = p.sbuf("gmax", [128, 8], F32)
    gcol = p.sbuf("gcol", [128, 2], F32)
    gtile = p.sbuf("gtile", [128, TTL], BF16)
    util = p.sbuf("util", [128, TTL], BF16)
    xs = ln_bufs["xs"]
    negrm = ln_bufs["negrm"]
    rows = ln_bufs["rows"]
    store_deps = []

    for half in range(2):
        H0 = half * HS
        for tt in range(NT):
            g0 = H0 + tt * TTL
            for k in range(NCH):
                p.dma("sync", xt[k][:], x_ap[k * 128:(k + 1) * 128, g0:g0 + TTL],
                      writes=[xt[k]])
            emit_ln(p, c, xt, ln_bufs, pool[4], pool[5], tt_len=TTL,
                    off=tt * TTL)
            rstd_row = rows[0:1, 3 * TTL:4 * TTL]
            rm_row = rows[0:1, 2 * TTL:3 * TTL]
            for sl in range(TTL // 128):
                t0 = tt * TTL + sl * 128
                raw = gsc[:, 0:8]
                lg = gsc[:, 8:16]
                wv = gsc[:, 16:24]
                for k in range(NCH):
                    p.op("tensor", lambda t, k=k, sl=sl: t.matmul(
                        pool[6][:, 0:8], xt[k][:, sl * 128:(sl + 1) * 128],
                        wg[k][:], start=(k == 0), stop=(k == NCH - 1)),
                        reads=[xt[k], wg[k]], writes=[pool[6]])
                p.op("tensor", lambda t, sl=sl: t.transpose(
                    pool[4][0:128, 0:1], rstd_row[0:1, sl * 128:(sl + 1) * 128],
                    c["ident_f"][0:1, 0:1]), reads=[rows, c["ident_f"]],
                    writes=[pool[4]])
                p.op("tensor", lambda t, sl=sl: t.transpose(
                    pool[5][0:128, 0:1], rm_row[0:1, sl * 128:(sl + 1) * 128],
                    c["ident_f"][0:1, 0:1]), reads=[rows, c["ident_f"]],
                    writes=[pool[5]])
                p.op("vector", lambda v: v.tensor_copy(gcol[:, 0:1],
                                                       pool[4][:, 0:1]),
                     reads=[pool[4]], writes=[gcol])
                p.op("vector", lambda v: v.tensor_copy(gcol[:, 1:2],
                                                       pool[5][:, 0:1]),
                     reads=[pool[5]], writes=[gcol])
                p.op("vector", lambda v: v.tensor_scalar(
                    raw, pool[6][:, 0:8], gcol[:, 0:1], None, op0=ALU.mult),
                    reads=[pool[6], gcol], writes=[gsc])
                p.op("vector", lambda v: v.tensor_scalar(
                    lg, cswg_nb[:], gcol[:, 1:2], None, op0=ALU.mult),
                    reads=[cswg_nb, gcol], writes=[gsc])
                p.op("vector", lambda v: v.tensor_tensor(lg, lg, raw, op=ALU.add),
                     reads=[gsc], writes=[gsc])
                p.op("scalar", lambda s: s.activation(lg, lg, AF.Sigmoid),
                     reads=[gsc], writes=[gsc])
                p.op("vector", lambda v: v.max(out=gmax[:], in_=lg),
                     reads=[gsc], writes=[gmax])
                p.op("vector", lambda v: v.tensor_scalar(
                    wv, lg, gmax[:, 1:2], None, op0=ALU.is_ge),
                    reads=[gsc, gmax], writes=[gsc])
                p.op("vector", lambda v: v.tensor_tensor(wv, wv, lg, op=ALU.mult),
                     reads=[gsc], writes=[gsc])
                p.op("vector", lambda v: v.reduce_sum(
                    gmax[:, 2:3], wv, axis=mybir.AxisListType.X),
                    reads=[gsc], writes=[gmax])
                p.op("vector", lambda v: v.reciprocal(gmax[:, 3:4], gmax[:, 2:3]),
                     reads=[gmax], writes=[gmax])
                p.op("vector", lambda v: v.tensor_scalar(
                    wv, wv, gmax[:, 3:4], None, op0=ALU.mult),
                    reads=[gsc, gmax], writes=[gsc])
                slg = t0 // 128
                p.op("vector", lambda v, slg=slg: v.tensor_copy(
                    w_tm[:, slg * 8:(slg + 1) * 8], wv),
                    reads=[gsc], writes=[w_tm])

        for e in range(-1, 8):
            if e < 0:
                load_expertlike(ws1_ap, ws3_ap, ws2_ap)
            else:
                load_expertlike(we1_ap, we3_ap, we2_ap, e)
                for sl in range(HS // 128):
                    p.op("tensor", lambda t, sl=sl, e=e: t.transpose(
                        pool[6][0:1, 0:128], w_tm[:, sl * 8 + e:sl * 8 + e + 1],
                        c["ident_f"][:]), reads=[w_tm, c["ident_f"]],
                        writes=[pool[6]])
                    p.op("vector", lambda v, sl=sl: v.tensor_copy(
                        w_row[0:1, sl * 128:(sl + 1) * 128], pool[6][0:1, 0:128]),
                        reads=[pool[6]], writes=[w_row])
            for tt in range(NT):
                sl = (tt * TTL, (tt + 1) * TTL)
                if e >= 0:
                    p.op("tensor", lambda t, sl=sl: t.matmul(
                        pool[6][:, 0:TTL], c["ones_full_f"][0:1, :],
                        w_row[0:1, sl[0]:sl[1]], start=True, stop=True),
                        reads=[w_row, c["ones_full_f"]], writes=[pool[6]])
                for j in range(NFJ):
                    emit_linear_acc(p, w1b, xs, negrm, pool[0], j * 128,
                                    (j + 1) * 128, tt_len=TTL, xs_sl=sl,
                                    nm_sl=sl)
                    emit_linear_acc(p, w3b, xs, negrm, pool[1], j * 128,
                                    (j + 1) * 128, tt_len=TTL, xs_sl=sl,
                                    nm_sl=sl)
                    p.op("scalar", lambda s: s.activation(
                        gtile[:], pool[0][:, 0:TTL], AF.Silu),
                        reads=[pool[0]], writes=[gtile])
                    p.op("vector", lambda v: v.tensor_copy(util[:],
                                                           pool[1][:, 0:TTL]),
                         reads=[pool[1]], writes=[util])
                    if e >= 0:
                        p.op("vector", lambda v: v.tensor_tensor(
                            util[:], util[:], pool[6][:, 0:TTL], op=ALU.mult),
                            reads=[util, pool[6]], writes=[util])
                    p.op("vector", lambda v, j=j: v.tensor_tensor(
                        h_bf[j][:], gtile[:], util[:], op=ALU.mult),
                        reads=[gtile, util], writes=[h_bf[j]])
                for o in range(NCH):
                    pb = pool[2 + (o % 2)]
                    for k in range(NFJ):
                        p.op("tensor", lambda t, k=k, o=o, pb=pb: t.matmul(
                            pb[:, 0:TTL], w2b[k][:, o * 128:(o + 1) * 128],
                            h_bf[k][:], start=(k == 0), stop=(k == NFJ - 1)),
                            reads=[w2b[k], h_bf[k]], writes=[pb])
                    if e < 0:
                        p.op("vector", lambda v, pb=pb, o=o, tt=tt: v.tensor_copy(
                            y_acc[o][:, tt * TTL:(tt + 1) * TTL], pb[:, 0:TTL]),
                            reads=[pb], writes=[y_acc[o]])
                    else:
                        p.op("vector", lambda v, pb=pb, o=o, tt=tt:
                             v.tensor_tensor(
                                 y_acc[o][:, tt * TTL:(tt + 1) * TTL],
                                 y_acc[o][:, tt * TTL:(tt + 1) * TTL],
                                 pb[:, 0:TTL], op=ALU.add),
                             reads=[pb, y_acc[o]], writes=[y_acc[o]])

        for tt in range(NT):
            g0 = H0 + tt * TTL
            for k in range(NCH):
                p.dma("sync", xt[k][:], x_ap[k * 128:(k + 1) * 128, g0:g0 + TTL],
                      writes=[xt[k]])
                p.op("vector", lambda v, k=k, tt=tt: v.scalar_tensor_tensor(
                    xt[k][:], y_acc[k][:, tt * TTL:(tt + 1) * TTL], 0.5 / 3.0,
                    xt[k][:], op0=ALU.mult, op1=ALU.add),
                    reads=[y_acc[k], xt[k]], writes=[xt[k]])
                store_deps.append(p.dma(
                    "scalar", y_ap[k * 128:(k + 1) * 128, g0:g0 + TTL],
                    xt[k][:], reads=[xt[k]]))

    return store_deps[-NCH:]


def build_l0_moe(TTL=512):
    nc = bass.Bass("TRN2", target_bir_lowering=False, debug=False)
    p = Prog(nc)
    FF = 2048
    NFJ = FF // 128
    HS = S // 2
    NT = HS // TTL
    x_ap = nc.dram_tensor("x", [D, S], F32, kind="ExternalInput").ap()
    wg_ap = nc.dram_tensor("Wg", [D, 8], F32, kind="ExternalInput").ap()
    ws1_ap = nc.dram_tensor("Ws1", [D, FF], F32, kind="ExternalInput").ap()
    ws3_ap = nc.dram_tensor("Ws3", [D, FF], F32, kind="ExternalInput").ap()
    ws2_ap = nc.dram_tensor("Ws2", [FF, D], F32, kind="ExternalInput").ap()
    we1_ap = nc.dram_tensor("We1", [8, D, FF], F32, kind="ExternalInput").ap()
    we3_ap = nc.dram_tensor("We3", [8, D, FF], F32, kind="ExternalInput").ap()
    we2_ap = nc.dram_tensor("We2", [8, FF, D], F32, kind="ExternalInput").ap()
    y_ap = nc.dram_tensor("y", [D, S], F32, kind="ExternalOutput").ap()

    c = emit_consts(p)
    sd = emit_l0_moe(p, c, {"x": x_ap, "Wg": wg_ap, "Ws1": ws1_ap, "Ws3": ws3_ap, "Ws2": ws2_ap, "We1": we1_ap, "We3": we3_ap, "We2": we2_ap, "y": y_ap}, TTL=TTL)
    return p.build(sd), ["x", "Wg", "Ws1", "Ws3", "Ws2",
                                        "We1", "We3", "We2"]


def emit_l3_moe(p, c, aps, TTL=512):
    x_ap = aps["x"]
    gw_ap = aps["GW"]
    gp_ap = aps["GP"]
    up_ap = aps["UP"]
    dp_ap = aps["DP"]
    pm_ap = aps["prevmask"]
    y_ap = aps["y"]
    cc_in = p.dram("cc_in", [1, 64], F32)
    cc_out = p.dram("cc_out", [8, 64], F32)
    FF = 2048
    NFJ = FF // 128
    HS = S // 4
    NT = HS // TTL
    NSL = S // 128
    CAP = float(int(1.25 * (NCORES * S) / 8))
    pool = [p.psum(f"pp{i}") for i in range(8)]

    wg = [p.sbuf(f"wg_k{k}", [128, 8], F32) for k in range(NCH)]
    for k in range(NCH):
        p.dma("sync", wg[k][:], gw_ap[k * 128:(k + 1) * 128, :], writes=[wg[k]])
    cswg_row = p.sbuf("cswg_row", [1, 8], F32)
    cswg_nb = p.sbuf("cswg_nb", [128, 8], F32)
    for k in range(NCH):
        p.op("tensor", lambda t, k=k: t.matmul(
            pool[6][0:1, 0:8], c["ones_full_f"][:, 0:1], wg[k][:],
            start=(k == 0), stop=(k == NCH - 1)),
            reads=[wg[k], c["ones_full_f"]], writes=[pool[6]])
    p.op("vector", lambda v: v.tensor_copy(cswg_row[:], pool[6][0:1, 0:8]),
         reads=[pool[6]], writes=[cswg_row])
    p.op("tensor", lambda t: t.matmul(pool[6][:, 0:8], c["ones_full_f"][0:1, :],
                                      cswg_row[:], start=True, stop=True),
         reads=[cswg_row, c["ones_full_f"]], writes=[pool[6]])
    p.op("vector", lambda v: v.tensor_scalar_mul(cswg_nb[:], pool[6][:, 0:8], -1.0),
         reads=[pool[6]], writes=[cswg_nb])
    prevm = p.sbuf("prevm", [8, 1], F32)
    p.dma("sync", prevm[:], pm_ap, writes=[prevm])

    w1b = {"nch": NCH, "tiles": [p.sbuf(f"w1b_k{k}", [128, FF], BF16)
                                 for k in range(NCH)],
           "csw": p.sbuf("w1b_csw", [1, FF], BF16)}
    w3b = {"nch": NCH, "tiles": [p.sbuf(f"w3b_k{k}", [128, FF], BF16)
                                 for k in range(NCH)],
           "csw": p.sbuf("w3b_csw", [1, FF], BF16)}
    w2b = [p.sbuf(f"w2b_k{k}", [128, D], BF16) for k in range(NFJ)]

    def load_expert(e):
        for k in range(NCH):
            p.dma("gpsimd", w1b["tiles"][k][:], gp_ap[e, k * 128:(k + 1) * 128, :],
                  writes=[w1b["tiles"][k]])
            p.dma("gpsimd", w3b["tiles"][k][:], up_ap[e, k * 128:(k + 1) * 128, :],
                  writes=[w3b["tiles"][k]])
        for k in range(NFJ):
            p.dma("gpsimd", w2b[k][:], dp_ap[e, k * 128:(k + 1) * 128, :],
                  writes=[w2b[k]])
        for wd in (w1b, w3b):
            for s_ in range(FF // 512):
                c0, c1 = s_ * 512, (s_ + 1) * 512
                for k in range(NCH):
                    p.op("tensor", lambda t, wd=wd, k=k, c0=c0, c1=c1: t.matmul(
                        pool[7][0:1, 0:512], c["ones_col_bf"][:],
                        wd["tiles"][k][:, c0:c1], start=(k == 0),
                        stop=(k == NCH - 1)),
                        reads=[wd["tiles"][k], c["ones_col_bf"]],
                        writes=[pool[7]])
                p.op("vector", lambda v, wd=wd, c0=c0, c1=c1: v.tensor_copy(
                    wd["csw"][0:1, c0:c1], pool[7][0:1, 0:512]),
                    reads=[pool[7]], writes=[wd["csw"]])

    h_bf = [p.sbuf(f"hbf_{j}", [128, TTL], BF16) for j in range(NFJ)]
    ln_bufs = alloc_ln_bufs(p, tt_len=TTL, xs_width=HS, xb=h_bf[0:NCH])
    y_acc = [p.sbuf(f"yacc_{o}", [128, HS], F32) for o in range(NCH)]
    comb_tm = p.sbuf("comb_tm", [128, 8 * NSL], F32)
    m_tm = p.sbuf("m_tm", [128, 16 * NSL], F32)
    wcols = p.sbuf("wcols", [128, 2 * NSL], F32)
    ind = p.sbuf("ind", [64, S], F32)
    incl = p.sbuf("incl", [64, S], F32)
    w_row = p.sbuf("w_row", [1, HS], F32)
    xt = [p.sbuf(f"xt_{k}", [128, TTL], F32) for k in range(NCH)]
    gsc = p.sbuf("gsc", [128, 3 * 8], F32)
    gmax = p.sbuf("gmax", [128, 8], F32)
    gcol = p.sbuf("gcol", [128, 2], F32)
    gtile = p.sbuf("gtile", [128, TTL], BF16)
    util = p.sbuf("util", [128, TTL], BF16)
    kscr = p.sbuf("kscr", [64, 128], F32)
    ccg = p.sbuf("ccg", [8, 64], F32)
    offs_row = p.sbuf("offs_row", [1, 64], F32)
    offs_col = p.sbuf("offs_col", [64, 1], F32)
    xs = ln_bufs["xs"]
    negrm = ln_bufs["negrm"]
    rows = ln_bufs["rows"]
    store_deps = []

    p.op("vector", lambda v: v.memset(ind[:], 0.0), writes=[ind])
    for tt in range(S // TTL):
        g0 = tt * TTL
        for k in range(NCH):
            p.dma("sync", xt[k][:], x_ap[k * 128:(k + 1) * 128, g0:g0 + TTL],
                  writes=[xt[k]])
        emit_ln(p, c, xt, ln_bufs, pool[4], pool[5], tt_len=TTL, skip_xs=True)
        rstd_row = rows[0:1, 3 * TTL:4 * TTL]
        rm_row = rows[0:1, 2 * TTL:3 * TTL]
        for sl in range(TTL // 128):
            slg = tt * (TTL // 128) + sl
            raw = gsc[:, 0:8]
            lg = gsc[:, 8:16]
            wv = gsc[:, 16:24]
            for k in range(NCH):
                p.op("tensor", lambda t, k=k, sl=sl: t.matmul(
                    pool[6][:, 0:8], xt[k][:, sl * 128:(sl + 1) * 128],
                    wg[k][:], start=(k == 0), stop=(k == NCH - 1)),
                    reads=[xt[k], wg[k]], writes=[pool[6]])
            p.op("tensor", lambda t, sl=sl: t.transpose(
                pool[4][0:128, 0:1], rstd_row[0:1, sl * 128:(sl + 1) * 128],
                c["ident_f"][0:1, 0:1]), reads=[rows, c["ident_f"]],
                writes=[pool[4]])
            p.op("tensor", lambda t, sl=sl: t.transpose(
                pool[5][0:128, 0:1], rm_row[0:1, sl * 128:(sl + 1) * 128],
                c["ident_f"][0:1, 0:1]), reads=[rows, c["ident_f"]],
                writes=[pool[5]])
            p.op("vector", lambda v: v.tensor_copy(gcol[:, 0:1], pool[4][:, 0:1]),
                 reads=[pool[4]], writes=[gcol])
            p.op("vector", lambda v: v.tensor_copy(gcol[:, 1:2], pool[5][:, 0:1]),
                 reads=[pool[5]], writes=[gcol])
            p.op("vector", lambda v: v.tensor_scalar(
                raw, pool[6][:, 0:8], gcol[:, 0:1], None, op0=ALU.mult),
                reads=[pool[6], gcol], writes=[gsc])
            p.op("vector", lambda v: v.tensor_scalar(
                lg, cswg_nb[:], gcol[:, 1:2], None, op0=ALU.mult),
                reads=[cswg_nb, gcol], writes=[gsc])
            p.op("vector", lambda v: v.tensor_tensor(lg, lg, raw, op=ALU.add),
                 reads=[gsc], writes=[gsc])
            p.op("vector", lambda v: v.reduce_max(
                gmax[:, 0:1], lg, axis=mybir.AxisListType.X, negate=True),
                reads=[gsc], writes=[gmax])
            p.op("scalar", lambda s: s.activation(
                lg, lg, AF.Exp, bias=gmax[:, 0:1], accum_out=gmax[:, 2:3]),
                reads=[gsc, gmax], writes=[gsc, gmax])
            p.op("vector", lambda v: v.reciprocal(gmax[:, 3:4], gmax[:, 2:3]),
                 reads=[gmax], writes=[gmax])
            p.op("vector", lambda v: v.tensor_scalar(
                lg, lg, gmax[:, 3:4], None, op0=ALU.mult),
                reads=[gsc, gmax], writes=[gsc])
            p.op("vector", lambda v: v.max(out=gmax[:], in_=lg),
                 reads=[gsc], writes=[gmax])
            p.op("vector", lambda v: v.tensor_scalar(
                gcol[:, 0:1], gmax[:, 0:1], gmax[:, 1:2], 1e-9,
                op0=ALU.add, op1=ALU.add), reads=[gmax], writes=[gcol])
            p.op("vector", lambda v: v.reciprocal(gcol[:, 1:2], gcol[:, 0:1]),
                 reads=[gcol], writes=[gcol])
            p.op("vector", lambda v, slg=slg: v.tensor_scalar(
                wcols[:, 2 * slg:2 * slg + 1], gmax[:, 0:1], gcol[:, 1:2],
                None, op0=ALU.mult), reads=[gmax, gcol], writes=[wcols])
            p.op("vector", lambda v, slg=slg: v.tensor_scalar(
                wcols[:, 2 * slg + 1:2 * slg + 2], gmax[:, 1:2], gcol[:, 1:2],
                None, op0=ALU.mult), reads=[gmax, gcol], writes=[wcols])
            m1 = m_tm[:, 16 * slg:16 * slg + 8]
            m2 = m_tm[:, 16 * slg + 8:16 * slg + 16]
            p.op("vector", lambda v, m1=m1: v.tensor_scalar(
                m1, lg, gmax[:, 0:1], None, op0=ALU.is_ge),
                reads=[gsc, gmax], writes=[m_tm])
            p.op("vector", lambda v, m2=m2: v.tensor_scalar(
                m2, lg, gmax[:, 1:2], None, op0=ALU.is_ge),
                reads=[gsc, gmax], writes=[m_tm])
            p.op("vector", lambda v, m1=m1, m2=m2: v.tensor_tensor(
                m2, m2, m1, op=ALU.subtract), reads=[m_tm], writes=[m_tm])
            p.op("tensor", lambda t, m1=m1: t.transpose(
                pool[6][0:8, 0:128], m1, c["ident_f"][:]),
                reads=[m_tm, c["ident_f"]], writes=[pool[6]])
            p.op("vector", lambda v, slg=slg: v.tensor_copy(
                ind[0:8, slg * 128:(slg + 1) * 128], pool[6][0:8, 0:128]),
                reads=[pool[6]], writes=[ind])
            p.op("tensor", lambda t, m2=m2: t.transpose(
                pool[6][0:8, 0:128], m2, c["ident_f"][:]),
                reads=[m_tm, c["ident_f"]], writes=[pool[6]])
            p.op("vector", lambda v, slg=slg: v.tensor_copy(
                ind[32:40, slg * 128:(slg + 1) * 128], pool[6][0:8, 0:128]),
                reads=[pool[6]], writes=[ind])

    p.op("vector", lambda v: v.memset(kscr[:], 1.0), writes=[kscr])
    p.op("vector", lambda v: v.tensor_tensor_scan(
        incl[0:64, :], kscr[:, 0:1].to_broadcast([64, S]), ind[0:64, :], 0.0,
        op0=ALU.mult, op1=ALU.add), reads=[ind, kscr], writes=[incl])
    p.dma("sync", cc_in.rearrange("o (p e) -> (o p) e", p=64),
          incl[0:64, S - 1:S], reads=[incl], writes=[cc_in])
    p.op("gpsimd", lambda g: g.collective_compute(
        "AllGather", ALU.bypass, replica_groups=[list(range(8))],
        ins=[cc_in.opt()], outs=[cc_out.opt()]),
        reads=[cc_in], writes=[cc_out])
    p.dma("sync", ccg[:], cc_out, reads=[cc_out], writes=[ccg])
    p.op("tensor", lambda t: t.matmul(
        pool[6][0:1, 0:64], prevm[:], ccg[:],
        start=True, stop=True), reads=[prevm, ccg], writes=[pool[6]])
    p.op("vector", lambda v: v.tensor_copy(offs_row[:], pool[6][0:1, 0:64]),
         reads=[pool[6]], writes=[offs_row])
    p.op("tensor", lambda t: t.transpose(
        pool[6][0:64, 0:1], offs_row[:], c["ident_f"][0:1, 0:1]),
        reads=[offs_row, c["ident_f"]], writes=[pool[6]])
    p.op("vector", lambda v: v.tensor_copy(offs_col[:], pool[6][0:64, 0:1]),
         reads=[pool[6]], writes=[offs_col])
    for slg in range(NSL):
        m1 = m_tm[:, 16 * slg:16 * slg + 8]
        m2 = m_tm[:, 16 * slg + 8:16 * slg + 16]
        p.op("vector", lambda v, slg=slg: v.tensor_scalar(
            kscr[:], incl[0:64, slg * 128:(slg + 1) * 128], offs_col[:], CAP,
            op0=ALU.add, op1=ALU.is_le), reads=[incl, offs_col], writes=[kscr])
        p.op("tensor", lambda t: t.transpose(
            pool[6][0:128, 0:64], kscr[:], c["ident_f"][0:64, 0:64]),
            reads=[kscr, c["ident_f"]], writes=[pool[6]])
        a = gsc[:, 0:8]
        b = gsc[:, 8:16]
        p.op("vector", lambda v, m1=m1, a=a: v.tensor_tensor(
            a, m1, pool[6][:, 0:8], op=ALU.mult),
            reads=[m_tm, pool[6]], writes=[gsc])
        p.op("vector", lambda v, a=a, slg=slg: v.tensor_scalar(
            a, a, wcols[:, 2 * slg:2 * slg + 1], None, op0=ALU.mult),
            reads=[gsc, wcols], writes=[gsc])
        p.op("vector", lambda v, m2=m2, b=b: v.tensor_tensor(
            b, m2, pool[6][:, 32:40], op=ALU.mult),
            reads=[m_tm, pool[6]], writes=[gsc])
        p.op("vector", lambda v, b=b, slg=slg: v.tensor_scalar(
            b, b, wcols[:, 2 * slg + 1:2 * slg + 2], None, op0=ALU.mult),
            reads=[gsc, wcols], writes=[gsc])
        p.op("vector", lambda v, a=a, b=b, slg=slg: v.tensor_tensor(
            comb_tm[:, 8 * slg:8 * (slg + 1)], a, b, op=ALU.add),
            reads=[gsc], writes=[comb_tm])

    for half in range(4):
        H0 = half * HS
        for tt in range(NT):
            g0 = H0 + tt * TTL
            for k in range(NCH):
                p.dma("sync", xt[k][:], x_ap[k * 128:(k + 1) * 128, g0:g0 + TTL],
                      writes=[xt[k]])
            emit_ln(p, c, xt, ln_bufs, pool[4], pool[5], tt_len=TTL,
                    off=tt * TTL)
        for e in range(8):
            load_expert(e)
            for sl in range(HS // 128):
                slg = half * (HS // 128) + sl
                p.op("tensor", lambda t, slg=slg, e=e: t.transpose(
                    pool[6][0:1, 0:128],
                    comb_tm[:, slg * 8 + e:slg * 8 + e + 1],
                    c["ident_f"][:]), reads=[comb_tm, c["ident_f"]],
                    writes=[pool[6]])
                p.op("vector", lambda v, sl=sl: v.tensor_copy(
                    w_row[0:1, sl * 128:(sl + 1) * 128], pool[6][0:1, 0:128]),
                    reads=[pool[6]], writes=[w_row])
            for tt in range(NT):
                sl = (tt * TTL, (tt + 1) * TTL)
                p.op("tensor", lambda t, sl=sl: t.matmul(
                    pool[6][:, 0:TTL], c["ones_full_f"][0:1, :],
                    w_row[0:1, sl[0]:sl[1]], start=True, stop=True),
                    reads=[w_row, c["ones_full_f"]], writes=[pool[6]])
                for j in range(NFJ):
                    emit_linear_acc(p, w1b, xs, negrm, pool[0], j * 128,
                                    (j + 1) * 128, tt_len=TTL, xs_sl=sl,
                                    nm_sl=sl)
                    emit_linear_acc(p, w3b, xs, negrm, pool[1], j * 128,
                                    (j + 1) * 128, tt_len=TTL, xs_sl=sl,
                                    nm_sl=sl)
                    p.op("scalar", lambda s: s.activation(
                        gtile[:], pool[0][:, 0:TTL], AF.Silu),
                        reads=[pool[0]], writes=[gtile])
                    p.op("vector", lambda v: v.tensor_copy(util[:],
                                                           pool[1][:, 0:TTL]),
                         reads=[pool[1]], writes=[util])
                    p.op("vector", lambda v: v.tensor_tensor(
                        util[:], util[:], pool[6][:, 0:TTL], op=ALU.mult),
                        reads=[util, pool[6]], writes=[util])
                    p.op("vector", lambda v, j=j: v.tensor_tensor(
                        h_bf[j][:], gtile[:], util[:], op=ALU.mult),
                        reads=[gtile, util], writes=[h_bf[j]])
                for o in range(NCH):
                    pb = pool[2 + (o % 2)]
                    for k in range(NFJ):
                        p.op("tensor", lambda t, k=k, o=o, pb=pb: t.matmul(
                            pb[:, 0:TTL], w2b[k][:, o * 128:(o + 1) * 128],
                            h_bf[k][:], start=(k == 0), stop=(k == NFJ - 1)),
                            reads=[w2b[k], h_bf[k]], writes=[pb])
                    if e == 0:
                        p.op("vector", lambda v, pb=pb, o=o, tt=tt: v.tensor_copy(
                            y_acc[o][:, tt * TTL:(tt + 1) * TTL], pb[:, 0:TTL]),
                            reads=[pb], writes=[y_acc[o]])
                    else:
                        p.op("vector", lambda v, pb=pb, o=o, tt=tt:
                             v.tensor_tensor(
                                 y_acc[o][:, tt * TTL:(tt + 1) * TTL],
                                 y_acc[o][:, tt * TTL:(tt + 1) * TTL],
                                 pb[:, 0:TTL], op=ALU.add),
                             reads=[pb, y_acc[o]], writes=[y_acc[o]])
        for tt in range(NT):
            g0 = H0 + tt * TTL
            for k in range(NCH):
                p.dma("sync", xt[k][:], x_ap[k * 128:(k + 1) * 128, g0:g0 + TTL],
                      writes=[xt[k]])
                p.op("vector", lambda v, k=k, tt=tt: v.scalar_tensor_tensor(
                    xt[k][:], y_acc[k][:, tt * TTL:(tt + 1) * TTL], 0.5,
                    xt[k][:], op0=ALU.mult, op1=ALU.add),
                    reads=[y_acc[k], xt[k]], writes=[xt[k]])
                store_deps.append(p.dma(
                    "scalar", y_ap[k * 128:(k + 1) * 128, g0:g0 + TTL],
                    xt[k][:], reads=[xt[k]]))

    return store_deps[-NCH:]


def build_l3_moe(TTL=512):
    nc = bass.Bass("TRN2", target_bir_lowering=False, debug=False)
    p = Prog(nc)
    FF = 2048
    NFJ = FF // 128
    HS = S // 4
    NT = HS // TTL
    NSL = S // 128
    CAP = float(int(1.25 * (NCORES * S) / 8))
    x_ap = nc.dram_tensor("x", [D, S], F32, kind="ExternalInput").ap()
    gw_ap = nc.dram_tensor("GW", [D, 8], F32, kind="ExternalInput").ap()
    gp_ap = nc.dram_tensor("GP", [8, D, FF], F32, kind="ExternalInput").ap()
    up_ap = nc.dram_tensor("UP", [8, D, FF], F32, kind="ExternalInput").ap()
    dp_ap = nc.dram_tensor("DP", [8, FF, D], F32, kind="ExternalInput").ap()
    pm_ap = nc.dram_tensor("prevmask", [8, 1], F32, kind="ExternalInput").ap()
    y_ap = nc.dram_tensor("y", [D, S], F32, kind="ExternalOutput").ap()

    c = emit_consts(p)
    sd = emit_l3_moe(p, c, {"x": x_ap, "GW": gw_ap, "GP": gp_ap, "UP": up_ap, "DP": dp_ap, "prevmask": pm_ap, "y": y_ap}, TTL=TTL)
    return p.build(sd), ["x", "GW", "GP", "UP", "DP", "prevmask"]


def emit_fp4_quant_free(p, out_bf, in_f32, scr, n_el, blocks_ap=None):
    n = n_el
    ng = n // 16
    a, bm, q1, q2 = scr["a"], scr["bm"], scr["q1"], scr["q2"]
    p.op("scalar", lambda s: s.activation(a[:, 0:n], in_f32, AF.Abs),
         reads=scr["in_r"], writes=[scr["ah"]])
    p.op("vector", lambda v: v.reduce_max(
        bm[:, 0:ng], a[:, 0:n].rearrange("p (g k) -> p g k", k=16),
        axis=mybir.AxisListType.X), reads=[scr["ah"]], writes=[scr["bmh"]])
    p.op("vector", lambda v: v.tensor_scalar(
        bm[:, 0:ng], bm[:, 0:ng], 1.0 / 6.0, 1e-12, op0=ALU.mult, op1=ALU.max),
        reads=[scr["bmh"]], writes=[scr["bmh"]])
    p.op("vector", lambda v: v.reciprocal(bm[:, ng:2 * ng], bm[:, 0:ng]),
         reads=[scr["bmh"]], writes=[scr["bmh"]])
    rb = bm[:, ng:2 * ng].rearrange("p (g o) -> p g o", o=1).to_broadcast(
        [128, ng, 16])
    p.op("vector", lambda v: v.tensor_tensor(
        a[:, 0:n].rearrange("p (g k) -> p g k", k=16),
        a[:, 0:n].rearrange("p (g k) -> p g k", k=16), rb, op=ALU.mult),
        reads=[scr["ah"], scr["bmh"]], writes=[scr["ah"]])
    p.op("vector", lambda v: v.tensor_scalar(
        q1[:, 0:n], a[:, 0:n], 2.0, MAGIC_RN, op0=ALU.mult, op1=ALU.add),
        reads=[scr["ah"]], writes=[scr["q1h"]])
    p.op("vector", lambda v: v.tensor_scalar(
        q1[:, 0:n], q1[:, 0:n], -MAGIC_RN, 4.0, op0=ALU.add, op1=ALU.min),
        reads=[scr["q1h"]], writes=[scr["q1h"]])
    p.op("vector", lambda v: v.tensor_scalar_add(q2[:, 0:n], a[:, 0:n], MAGIC_RN),
         reads=[scr["ah"]], writes=[scr["q2h"]])
    p.op("vector", lambda v: v.tensor_scalar(
        q2[:, 0:n], q2[:, 0:n], -(MAGIC_RN + 2.0), 0.0, op0=ALU.add, op1=ALU.max),
        reads=[scr["q2h"]], writes=[scr["q2h"]])
    p.op("vector", lambda v: v.tensor_scalar_min(q2[:, 0:n], q2[:, 0:n], 2.0),
         reads=[scr["q2h"]], writes=[scr["q2h"]])
    p.op("vector", lambda v: v.tensor_scalar(
        q1[:, 0:n], q1[:, 0:n], 0.5, None, op0=ALU.mult),
        reads=[scr["q1h"]], writes=[scr["q1h"]])
    p.op("vector", lambda v: v.tensor_tensor(q1[:, 0:n], q1[:, 0:n], q2[:, 0:n],
                                             op=ALU.add),
        reads=[scr["q1h"], scr["q2h"]], writes=[scr["q1h"]])
    p.op("vector", lambda v: v.tensor_scalar(
        q2[:, 0:n], a[:, 0:n], 5.0, 2.0, op0=ALU.is_gt, op1=ALU.mult),
        reads=[scr["ah"]], writes=[scr["q2h"]])
    p.op("vector", lambda v: v.tensor_tensor(q1[:, 0:n], q1[:, 0:n], q2[:, 0:n],
                                             op=ALU.add),
        reads=[scr["q1h"], scr["q2h"]], writes=[scr["q1h"]])
    p.op("scalar", lambda s: s.activation(q2[:, 0:n], in_f32, AF.Sign),
         reads=scr["in_r"], writes=[scr["q2h"]])
    p.op("vector", lambda v: v.tensor_tensor(q1[:, 0:n], q1[:, 0:n], q2[:, 0:n],
                                             op=ALU.mult),
        reads=[scr["q1h"], scr["q2h"]], writes=[scr["q1h"]])
    sb = bm[:, 0:ng].rearrange("p (g o) -> p g o", o=1).to_broadcast(
        [128, ng, 16])
    p.op("vector", lambda v: v.tensor_tensor(
        out_bf.rearrange("p (g k) -> p g k", k=16),
        q1[:, 0:n].rearrange("p (g k) -> p g k", k=16), sb, op=ALU.mult),
        reads=[scr["q1h"], scr["bmh"]], writes=scr["out_w"])


def emit_fp4_quant_part(p, out_bf, in_ap, scr, n, in_r, out_w):
    a, t = scr["a"], scr["q1"]
    p.op("scalar", lambda s: s.activation(a[:, 0:n], in_ap, AF.Abs),
         reads=in_r, writes=[scr["ah"]])
    for r in (1, 2, 4, 8):
        mask = [(i ^ r) for i in range(32)]
        p.op("vector", lambda v, mask=mask: v.stream_shuffle(
            t[:, 0:n], a[:, 0:n], mask), reads=[scr["ah"]], writes=[scr["q1h"]])
        p.op("vector", lambda v: v.tensor_tensor(a[:, 0:n], a[:, 0:n],
                                                 t[:, 0:n], op=ALU.max),
             reads=[scr["ah"], scr["q1h"]], writes=[scr["ah"]])
    p.op("vector", lambda v: v.tensor_scalar(
        scr["bm"][:, 0:n], a[:, 0:n], 1.0 / 6.0, 1e-12, op0=ALU.mult,
        op1=ALU.max), reads=[scr["ah"]], writes=[scr["bmh"]])
    p.op("vector", lambda v: v.reciprocal(a[:, 0:n], scr["bm"][:, 0:n]),
         reads=[scr["bmh"]], writes=[scr["ah"]])
    p.op("scalar", lambda s: s.activation(scr["q2"][:, 0:n], in_ap, AF.Abs),
         reads=in_r, writes=[scr["q2h"]])
    p.op("vector", lambda v: v.tensor_tensor(
        a[:, 0:n], a[:, 0:n], scr["q2"][:, 0:n], op=ALU.mult),
        reads=[scr["ah"], scr["q2h"]], writes=[scr["ah"]])
    q1, q2 = scr["q1"], scr["q2"]
    p.op("vector", lambda v: v.tensor_scalar(
        q1[:, 0:n], a[:, 0:n], 2.0, MAGIC_RN, op0=ALU.mult, op1=ALU.add),
        reads=[scr["ah"]], writes=[scr["q1h"]])
    p.op("vector", lambda v: v.tensor_scalar(
        q1[:, 0:n], q1[:, 0:n], -MAGIC_RN, 4.0, op0=ALU.add, op1=ALU.min),
        reads=[scr["q1h"]], writes=[scr["q1h"]])
    p.op("vector", lambda v: v.tensor_scalar(
        q1[:, 0:n], q1[:, 0:n], 0.5, None, op0=ALU.mult),
        reads=[scr["q1h"]], writes=[scr["q1h"]])
    p.op("vector", lambda v: v.tensor_scalar_add(q2[:, 0:n], a[:, 0:n], MAGIC_RN),
         reads=[scr["ah"]], writes=[scr["q2h"]])
    p.op("vector", lambda v: v.tensor_scalar(
        q2[:, 0:n], q2[:, 0:n], -(MAGIC_RN + 2.0), 0.0, op0=ALU.add, op1=ALU.max),
        reads=[scr["q2h"]], writes=[scr["q2h"]])
    p.op("vector", lambda v: v.tensor_scalar_min(q2[:, 0:n], q2[:, 0:n], 2.0),
         reads=[scr["q2h"]], writes=[scr["q2h"]])
    p.op("vector", lambda v: v.tensor_tensor(q1[:, 0:n], q1[:, 0:n], q2[:, 0:n],
                                             op=ALU.add),
        reads=[scr["q1h"], scr["q2h"]], writes=[scr["q1h"]])
    p.op("vector", lambda v: v.tensor_scalar(
        q2[:, 0:n], a[:, 0:n], 5.0, 2.0, op0=ALU.is_gt, op1=ALU.mult),
        reads=[scr["ah"]], writes=[scr["q2h"]])
    p.op("vector", lambda v: v.tensor_tensor(q1[:, 0:n], q1[:, 0:n], q2[:, 0:n],
                                             op=ALU.add),
        reads=[scr["q1h"], scr["q2h"]], writes=[scr["q1h"]])
    p.op("scalar", lambda s: s.activation(q2[:, 0:n], in_ap, AF.Sign),
         reads=in_r, writes=[scr["q2h"]])
    p.op("vector", lambda v: v.tensor_tensor(q1[:, 0:n], q1[:, 0:n], q2[:, 0:n],
                                             op=ALU.mult),
        reads=[scr["q1h"], scr["q2h"]], writes=[scr["q1h"]])
    p.op("vector", lambda v: v.tensor_tensor(out_bf, q1[:, 0:n],
                                             scr["bm"][:, 0:n], op=ALU.mult),
        reads=[scr["q1h"], scr["bmh"]], writes=out_w)


def emit_l2_ffn(p, c, aps, TTL=256):
    x_ap = aps["x"]
    wgt_ap = aps["Wgt"]
    wup_ap = aps["Wup"]
    wdn_ap = aps["Wdn"]
    y_ap = aps["y"]
    I = 2816
    NI = I // 128
    HS = S // 2
    NT = HS // TTL
    dq_gate = p.dram("dq_gate", [I, D], BF16)
    dq_up = p.dram("dq_up", [I, D], BF16)
    dq_dn = p.dram("dq_dn", [D, I], BF16)
    dq_h = p.dram("dq_h", [S, I], BF16)
    pool = [p.psum(f"pp{i}") for i in range(8)]

    QW = 1024
    qsrc = p.sbuf("qsrc", [128, QW], F32)
    qout = p.sbuf("qout", [128, QW], BF16)
    scr = {
        "a": p.sbuf("q_a", [128, QW], F32),
        "bm": p.sbuf("q_bm", [128, 2 * (QW // 16)], F32),
        "q1": p.sbuf("q_q1", [128, QW], F32),
        "q2": p.sbuf("q_q2", [128, QW], F32),
    }
    scr["ah"], scr["bmh"], scr["q1h"], scr["q2h"] = (
        scr["a"], scr["bm"], scr["q1"], scr["q2"])

    for (wap, dq, rows, cols) in ((wgt_ap, dq_gate, I, D),
                                  (wup_ap, dq_up, I, D),
                                  (wdn_ap, dq_dn, D, I)):
        for k in range((rows + 127) // 128):
            r0, r1 = k * 128, min(rows, (k + 1) * 128)
            pr = r1 - r0
            for s0 in range(0, cols, QW):
                s1 = min(cols, s0 + QW)
                p.dma("sync", qsrc[0:pr, 0:s1 - s0], wap[r0:r1, s0:s1],
                      writes=[qsrc])
                scr["in_r"] = [qsrc]
                scr["out_w"] = [qout]
                emit_fp4_quant_free(p, qout[0:pr, 0:s1 - s0],
                                    qsrc[0:pr, 0:s1 - s0], scr, s1 - s0)
                p.dma("scalar", dq[r0:r1, s0:s1], qout[0:pr, 0:s1 - s0],
                      reads=[qout], writes=[dq])

    wqg = [p.sbuf(f"wqg_{o}", [128, I], BF16) for o in range(NCH)]
    wqu = [p.sbuf(f"wqu_{o}", [128, I], BF16) for o in range(NCH)]
    wqd = [p.sbuf(f"wqd_{k}", [128, D], BF16) for k in range(NI)]
    for o in range(NCH):
        p.op("sync", lambda e, o=o: e.dma_start_transpose(
            out=wqg[o][:], in_=dq_gate[:, o * 128:(o + 1) * 128]),
            reads=[dq_gate], writes=[wqg[o]], dma=True)
        p.op("sync", lambda e, o=o: e.dma_start_transpose(
            out=wqu[o][:], in_=dq_up[:, o * 128:(o + 1) * 128]),
            reads=[dq_up], writes=[wqu[o]], dma=True)
    for k in range(NI):
        p.op("sync", lambda e, k=k: e.dma_start_transpose(
            out=wqd[k][:], in_=dq_dn[:, k * 128:(k + 1) * 128]),
            reads=[dq_dn], writes=[wqd[k]], dma=True)

    ln_bufs = alloc_ln_bufs(p, tt_len=TTL, xs_width=TTL)
    rrowf = p.sbuf("rrowf", [1, TTL], F32)
    nmrowf = p.sbuf("nmrowf", [1, TTL], F32)
    xq = [p.sbuf(f"xq_{k}", [128, HS], BF16) for k in range(NCH)]
    xt = [p.sbuf(f"xt_{k}", [128, TTL], F32) for k in range(NCH)]
    xlnf = p.sbuf("xlnf", [128, TTL], F32)
    h_tm = p.sbuf("h_tm", [128, QW], F32)
    hq_tm = p.sbuf("hq_tm", [128, QW], BF16)
    hqk = [p.sbuf(f"hqk_{i}", [128, TTL], BF16) for i in range(2)]
    store_deps = []

    for half in range(2):
        H0 = half * HS
        for tt in range(NT):
            g0 = H0 + tt * TTL
            for k in range(NCH):
                p.dma("sync", xt[k][:], x_ap[k * 128:(k + 1) * 128, g0:g0 + TTL],
                      writes=[xt[k]])
            emit_ln(p, c, xt, ln_bufs, pool[4], pool[5], tt_len=TTL,
                    skip_xs=True, rrow_f=rrowf, nmrow_f=nmrowf)
            p.op("tensor", lambda t: t.matmul(
                pool[6][:, 0:TTL], c["ones_full_f"][0:1, :], rrowf[0:1, 0:TTL],
                start=True, stop=True),
                reads=[rrowf, c["ones_full_f"]], writes=[pool[6]])
            p.op("tensor", lambda t: t.matmul(
                pool[7][:, 0:TTL], c["ones_full_f"][0:1, :], nmrowf[0:1, 0:TTL],
                start=True, stop=True),
                reads=[nmrowf, c["ones_full_f"]], writes=[pool[7]])
            for k in range(NCH):
                p.op("vector", lambda v, k=k: v.tensor_tensor(
                    xlnf[:], xt[k][:], pool[6][:, 0:TTL], op=ALU.mult),
                    reads=[pool[6], xt[k]], writes=[xlnf])
                p.op("vector", lambda v, k=k: v.tensor_tensor(
                    xlnf[:], xlnf[:], pool[7][:, 0:TTL], op=ALU.add),
                    reads=[pool[7], xlnf], writes=[xlnf])
                scr2 = dict(scr)
                scr2["bm"] = qsrc
                scr2["bmh"] = qsrc
                scr2["in_r"] = [xlnf]
                scr2["out_w"] = [xq[k]]
                emit_fp4_quant_part(
                    p, xq[k][:, tt * TTL:(tt + 1) * TTL], xlnf[:], scr2, TTL,
                    [xlnf], [xq[k]])
        for sl in range(HS // 128):
            t0g = H0 + sl * 128
            t0l = sl * 128
            for slab0 in range(0, I, QW):
                slab1 = min(I, slab0 + QW)
                for f0 in range(slab0, slab1, 512):
                    f1 = min(slab1, f0 + 512)
                    pb = pool[(f0 // 512) % 2]
                    pu = pool[2 + (f0 // 512) % 2]
                    for k in range(NCH):
                        p.op("tensor", lambda t, k=k, f0=f0, f1=f1, pb=pb,
                             t0l=t0l: t.matmul(
                                 pb[0:128, 0:f1 - f0], xq[k][:, t0l:t0l + 128],
                                 wqg[k][:, f0:f1], start=(k == 0),
                                 stop=(k == NCH - 1)),
                             reads=[xq[k], wqg[k]], writes=[pb])
                    for k in range(NCH):
                        p.op("tensor", lambda t, k=k, f0=f0, f1=f1, pu=pu,
                             t0l=t0l: t.matmul(
                                 pu[0:128, 0:f1 - f0], xq[k][:, t0l:t0l + 128],
                                 wqu[k][:, f0:f1], start=(k == 0),
                                 stop=(k == NCH - 1)),
                             reads=[xq[k], wqu[k]], writes=[pu])
                    p.op("scalar", lambda s, f0=f0, f1=f1, pb=pb, slab0=slab0:
                         s.activation(h_tm[:, f0 - slab0:f1 - slab0],
                                      pb[0:128, 0:f1 - f0], AF.Silu),
                         reads=[pb], writes=[h_tm])
                    p.op("vector", lambda v, f0=f0, f1=f1, pu=pu, slab0=slab0:
                         v.tensor_tensor(h_tm[:, f0 - slab0:f1 - slab0],
                                         h_tm[:, f0 - slab0:f1 - slab0],
                                         pu[0:128, 0:f1 - f0], op=ALU.mult),
                         reads=[h_tm, pu], writes=[h_tm])
                scr["in_r"] = [h_tm]
                scr["out_w"] = [hq_tm]
                emit_fp4_quant_free(p, hq_tm[0:128, 0:slab1 - slab0],
                                    h_tm[0:128, 0:slab1 - slab0], scr,
                                    slab1 - slab0)
                p.dma("scalar", dq_h[t0g:t0g + 128, slab0:slab1],
                      hq_tm[0:128, 0:slab1 - slab0], reads=[hq_tm],
                      writes=[dq_h])
        for tt in range(NT):
            g0 = H0 + tt * TTL
            for k in range(NI):
                hb = hqk[k % 2]
                p.op("sync", lambda e, k=k, g0=g0, hb=hb: e.dma_start_transpose(
                    out=hb[:], in_=dq_h[g0:g0 + TTL, k * 128:(k + 1) * 128]),
                    reads=[dq_h], writes=[hb], dma=True)
                for o in range(NCH):
                    p.op("tensor", lambda t, k=k, o=o, hb=hb: t.matmul(
                        pool[o][:, 0:TTL], wqd[k][:, o * 128:(o + 1) * 128],
                        hb[:], start=(k == 0), stop=(k == NI - 1)),
                        reads=[wqd[k], hb], writes=[pool[o]])
            for o in range(NCH):
                p.dma("gpsimd", xt[o][:], x_ap[o * 128:(o + 1) * 128, g0:g0 + TTL],
                      writes=[xt[o]])
                p.op("vector", lambda v, o=o: v.scalar_tensor_tensor(
                    xt[o][:], pool[o][:, 0:TTL], 0.5, xt[o][:],
                    op0=ALU.mult, op1=ALU.add),
                    reads=[pool[o], xt[o]], writes=[xt[o]])
                store_deps.append(p.dma(
                    "scalar", y_ap[o * 128:(o + 1) * 128, g0:g0 + TTL],
                    xt[o][:], reads=[xt[o]]))

    return store_deps[-NCH:]


def build_l2_ffn(TTL=256):
    nc = bass.Bass("TRN2", target_bir_lowering=False, debug=False)
    p = Prog(nc)
    I = 2816
    NI = I // 128
    HS = S // 2
    NT = HS // TTL
    x_ap = nc.dram_tensor("x", [D, S], F32, kind="ExternalInput").ap()
    wgt_ap = nc.dram_tensor("Wgt", [I, D], F32, kind="ExternalInput").ap()
    wup_ap = nc.dram_tensor("Wup", [I, D], F32, kind="ExternalInput").ap()
    wdn_ap = nc.dram_tensor("Wdn", [D, I], F32, kind="ExternalInput").ap()
    y_ap = nc.dram_tensor("y", [D, S], F32, kind="ExternalOutput").ap()

    c = emit_consts(p)
    sd = emit_l2_ffn(p, c, {"x": x_ap, "Wgt": wgt_ap, "Wup": wup_ap, "Wdn": wdn_ap, "y": y_ap}, TTL=TTL)
    return p.build(sd), ["x", "Wgt", "Wup", "Wdn"]


def emit_l0_attn(p, c, aps, TTL=512):
    x_ap = aps["x"]
    wq_ap = aps["Wq"]
    wk_ap = aps["Wk"]
    wv_ap = aps["Wv"]
    wo_ap = aps["Wo"]
    bq_ap = aps["bq"]
    bk_ap = aps["bk"]
    bv_ap = aps["bv"]
    bo_ap = aps["bo"]
    y_ap = aps["y"]
    H = 16
    C = 128
    NC = S // C
    NT = S // TTL
    pool = [p.psum(f"pp{i}") for i in range(6)]

    brow = {}
    for nm, ap in (("bq", bq_ap), ("bk", bk_ap), ("bv", bv_ap), ("bo", bo_ap)):
        brow[nm] = p.sbuf(f"{nm}_row", [1, D], BF16)
        p.dma("gpsimd", brow[nm][:], ap, writes=[brow[nm]])

    wbuf = [p.sbuf(f"wbuf_{k}", [128, D], BF16) for k in range(NCH)]
    wcs = p.sbuf("wcs", [1, D], BF16)

    def load_w(ap, colsum):
        for k in range(NCH):
            p.dma("gpsimd", wbuf[k][:], ap[k * 128:(k + 1) * 128, :],
                  writes=[wbuf[k]])
        if colsum:
            for s_ in range(D // 512):
                c0, c1 = s_ * 512, (s_ + 1) * 512
                for k in range(NCH):
                    p.op("tensor", lambda t, k=k, c0=c0, c1=c1: t.matmul(
                        pool[3][0:1, 0:512], c["ones_col_bf"][:],
                        wbuf[k][:, c0:c1], start=(k == 0), stop=(k == NCH - 1)),
                        reads=[wbuf[k], c["ones_col_bf"]], writes=[pool[3]])
                p.op("vector", lambda v, c0=c0, c1=c1: v.tensor_copy(
                    wcs[0:1, c0:c1], pool[3][0:1, 0:512]),
                    reads=[pool[3]], writes=[wcs])

    v_tm = [p.sbuf(f"vtm_{i}", [128, D], BF16) for i in range(NC)]
    ln_bufs = alloc_ln_bufs(p, tt_len=TTL, xs_width=S, xb=v_tm[0:NCH])
    xs = ln_bufs["xs"]
    negrm = ln_bufs["negrm"]
    xt = [p.sbuf(f"xt_{k}", [128, TTL], F32) for k in range(NCH)]
    k_tm = [p.sbuf(f"ktm_{i}", [128, D], BF16) for i in range(NC)]
    qg = [p.sbuf(f"qg_{i}", [64, S], BF16) for i in range(1)]
    kg = [p.sbuf(f"kg_{i}", [64, S], BF16) for i in range(1)]
    vals = [p.sbuf(f"vals_{k}", [128, S], BF16) for k in range(NCH)]
    wsm_q = [p.sbuf(f"wsmq_{k}", [128, 64], BF16) for k in range(NCH)]
    wsm_k = [p.sbuf(f"wsmk_{k}", [128, 64], BF16) for k in range(NCH)]
    csq = p.sbuf("csq", [1, 64], BF16)
    csk = p.sbuf("csk", [1, 64], BF16)
    esc = p.sbuf("esc", [128, 2 * TTL], BF16)
    emk = p.sbuf("emk", [128, TTL], U8)
    S_sb = p.sbuf("S_sb", [64, 128], BF16)
    S_f32 = p.sbuf("S_f32", [64, 128], F32)
    pt_bf = p.sbuf("pt_bf", [128, C], BF16)
    of_bf = p.sbuf("of_bf", [64, C], BF16)
    zr = p.sbuf("zr", [1, 2 * C], F32)
    store_deps = []

    for tt in range(NT):
        g0 = tt * TTL
        for k in range(NCH):
            p.dma("sync", xt[k][:], x_ap[k * 128:(k + 1) * 128, g0:g0 + TTL],
                  writes=[xt[k]])
        emit_ln(p, c, xt, ln_bufs, pool[3], pool[4], tt_len=TTL, off=g0)

    def elu_evac(psum_t, psum_ap, out_h, out_ap, n, do_elu):
        if not do_elu:
            p.op("vector", lambda v: v.tensor_copy(out_ap, psum_ap),
                 reads=[psum_t], writes=[out_h])
            return
        e1 = esc[0:psum_ap.shape[0], 0:n]
        e2 = esc[0:psum_ap.shape[0], n:2 * n]
        p.op("scalar", lambda s: s.activation(e1, psum_ap, AF.Exp),
             reads=[psum_t], writes=[esc])
        p.op("vector", lambda v: v.tensor_scalar_add(e2, psum_ap, 1.0),
             reads=[psum_t], writes=[esc])
        p.op("vector", lambda v: v.tensor_scalar(
            emk[0:psum_ap.shape[0], 0:n], psum_ap, 0.0, None, op0=ALU.is_gt),
            reads=[psum_t], writes=[emk])
        p.op("vector", lambda v: v.select(
            out_ap, emk[0:psum_ap.shape[0], 0:n], e2, e1, add_drain=True),
            reads=[esc, emk], writes=[out_h])

    for (wap, dst, bias, do_elu) in ((wv_ap, v_tm, "bv", False),
                                     (wk_ap, k_tm, "bk", True)):
        load_w(wap, True)
        for i in range(NC):
            t0 = i * C
            for vs in range(2):
                c0, c1 = vs * 512, (vs + 1) * 512
                pb = pool[vs]
                for k in range(NCH):
                    p.op("tensor", lambda t, k=k, c0=c0, c1=c1, pb=pb, t0=t0:
                         t.matmul(pb[:, 0:512], xs[k][:, t0:t0 + C],
                                  wbuf[k][:, c0:c1], start=(k == 0),
                                  stop=False),
                         reads=[xs[k], wbuf[k]], writes=[pb])
                p.op("tensor", lambda t, c0=c0, c1=c1, pb=pb, t0=t0: t.matmul(
                    pb[:, 0:512], negrm[0:1, t0:t0 + C], wcs[0:1, c0:c1],
                    start=False, stop=False),
                    reads=[negrm, wcs], writes=[pb])
                p.op("tensor", lambda t, c0=c0, c1=c1, pb=pb, bias=bias:
                     t.matmul(pb[:, 0:512], c["ones_row_bf"][:],
                              brow[bias][0:1, c0:c1], start=False, stop=True),
                     reads=[brow[bias], c["ones_row_bf"]], writes=[pb])
                elu_evac(pb, pb[:, 0:512], dst[i], dst[i][:, c0:c1], 512,
                         do_elu)

    for h in range(H):
        for k in range(NCH):
            p.dma("gpsimd", wsm_q[k][:],
                  wq_ap[k * 128:(k + 1) * 128, h * 64:(h + 1) * 64],
                  writes=[wsm_q[k]])
            p.dma("gpsimd", wsm_k[k][:],
                  wk_ap[k * 128:(k + 1) * 128, h * 64:(h + 1) * 64],
                  writes=[wsm_k[k]])
        for (wsm, cs) in ((wsm_q, csq), (wsm_k, csk)):
            for k in range(NCH):
                p.op("tensor", lambda t, k=k, wsm=wsm: t.matmul(
                    pool[3][0:1, 0:64], c["ones_col_bf"][:], wsm[k][:],
                    start=(k == 0), stop=(k == NCH - 1)),
                    reads=[wsm[k], c["ones_col_bf"]], writes=[pool[3]])
            p.op("vector", lambda v, cs=cs: v.tensor_copy(
                cs[0:1, :], pool[3][0:1, 0:64]), reads=[pool[3]], writes=[cs])
        for tt in range(NT):
            t0 = tt * TTL
            for (wsm, cs, bias, dstp) in ((wsm_q, csq, "bq", qg),
                                          (wsm_k, csk, "bk", kg)):
                pb = pool[4]
                for k in range(NCH):
                    p.op("tensor", lambda t, k=k, wsm=wsm, t0=t0, pb=pb:
                         t.matmul(pb[0:64, 0:TTL], wsm[k][:],
                                  xs[k][:, t0:t0 + TTL],
                                  start=(k == 0), stop=False),
                         reads=[xs[k], wsm[k]], writes=[pb])
                p.op("tensor", lambda t, cs=cs, t0=t0, pb=pb: t.matmul(
                    pb[0:64, 0:TTL], cs[0:1, :], negrm[0:1, t0:t0 + TTL],
                    start=False, stop=False), reads=[cs, negrm], writes=[pb])
                p.op("tensor", lambda t, bias=bias, pb=pb, h=h: t.matmul(
                    pb[0:64, 0:TTL], brow[bias][0:1, h * 64:(h + 1) * 64],
                    c["ones_row_bf4"][0:1, 0:TTL], start=False, stop=True),
                    reads=[brow[bias], c["ones_row_bf4"]], writes=[pb])
                elu_evac(pb, pb[0:64, 0:TTL], dstp[0],
                         dstp[0][0:64, t0:t0 + TTL], TTL, True)
        if True:
            hh = 0
            for ci in range(NC):
                t0 = ci * C
                hs = h * 64
                p.op("tensor", lambda t, hh=hh, t0=t0: t.matmul(
                    pool[0][:, 0:C], kg[hh][0:64, t0:t0 + C],
                    qg[hh][0:64, t0:t0 + C], start=True, stop=True),
                    reads=[kg[hh], qg[hh]], writes=[pool[0]])
                p.op("vector", lambda v: v.tensor_tensor(
                    pt_bf[:], pool[0][:, 0:C], c["mask_ut"][:], op=ALU.mult),
                    reads=[pool[0], c["mask_ut"]], writes=[pt_bf])
                if ci > 0:
                    p.op("vector", lambda v: v.tensor_copy(
                        S_sb[0:64, 0:65], S_f32[0:64, 0:65]),
                        reads=[S_f32], writes=[S_sb])
                    p.op("tensor", lambda t, hh=hh, t0=t0: t.matmul(
                        pool[1][0:64, 0:C], S_sb[0:64, 0:64],
                        qg[hh][0:64, t0:t0 + C], start=True, stop=False),
                        reads=[S_sb, qg[hh]], writes=[pool[1]])
                    p.op("tensor", lambda t, ci=ci, hs=hs: t.matmul(
                        pool[1][0:64, 0:C], v_tm[ci][:, hs:hs + 64],
                        pt_bf[:], start=False, stop=True),
                        reads=[v_tm[ci], pt_bf], writes=[pool[1]])
                    p.op("tensor", lambda t: t.matmul(
                        pool[2][0:1, 0:C], c["ones_col_bf"][:], pt_bf[:],
                        start=True, stop=False),
                        reads=[c["ones_col_bf"], pt_bf], writes=[pool[2]])
                    p.op("tensor", lambda t, hh=hh, t0=t0: t.matmul(
                        pool[2][0:1, 0:C], S_sb[0:64, 64:65],
                        qg[hh][0:64, t0:t0 + C], start=False, stop=True),
                        reads=[S_sb, qg[hh]], writes=[pool[2]])
                else:
                    p.op("tensor", lambda t, ci=ci, hs=hs: t.matmul(
                        pool[1][0:64, 0:C], v_tm[ci][:, hs:hs + 64],
                        pt_bf[:], start=True, stop=True),
                        reads=[v_tm[ci], pt_bf], writes=[pool[1]])
                    p.op("tensor", lambda t: t.matmul(
                        pool[2][0:1, 0:C], c["ones_col_bf"][:], pt_bf[:],
                        start=True, stop=True),
                        reads=[c["ones_col_bf"], pt_bf], writes=[pool[2]])
                p.op("vector", lambda v: v.tensor_scalar_add(
                    zr[0:1, 0:C], pool[2][0:1, 0:C], 1e-6),
                    reads=[pool[2]], writes=[zr])
                p.op("vector", lambda v: v.reciprocal(
                    zr[0:1, C:2 * C], zr[0:1, 0:C]), reads=[zr], writes=[zr])
                p.op("tensor", lambda t: t.matmul(
                    pool[3][0:64, 0:C], c["ones_full_f"][0:1, 0:64],
                    zr[0:1, C:2 * C], start=True, stop=True),
                    reads=[zr, c["ones_full_f"]], writes=[pool[3]])
                p.op("vector", lambda v: v.tensor_copy(
                    of_bf[0:64, 0:C], pool[1][0:64, 0:C]),
                    reads=[pool[1]], writes=[of_bf])
                p.op("vector", lambda v, h=h, t0=t0: v.tensor_tensor(
                    vals[h // 2][64 * (h % 2):64 * (h % 2) + 64, t0:t0 + C],
                    of_bf[0:64, 0:C], pool[3][0:64, 0:C], op=ALU.mult),
                    reads=[of_bf, pool[3]], writes=[vals[h // 2]])
                p.op("tensor", lambda t, ci=ci, hs=hs: t.matmul(
                    pool[4][0:64, 0:64], k_tm[ci][:, hs:hs + 64],
                    v_tm[ci][:, hs:hs + 64], start=True, stop=False),
                    reads=[k_tm[ci], v_tm[ci]], writes=[pool[4]])
                p.op("tensor", lambda t, ci=ci, hs=hs: t.matmul(
                    pool[4][0:64, 64:65], k_tm[ci][:, hs:hs + 64],
                    c["ones_col_bf"][:], start=False, stop=True),
                    reads=[k_tm[ci], c["ones_col_bf"]], writes=[pool[4]])
                if ci == 0:
                    p.op("vector", lambda v: v.tensor_copy(
                        S_f32[0:64, 0:65], pool[4][0:64, 0:65]),
                        reads=[pool[4]], writes=[S_f32])
                else:
                    p.op("vector", lambda v: v.tensor_tensor(
                        S_f32[0:64, 0:65], S_f32[0:64, 0:65],
                        pool[4][0:64, 0:65], op=ALU.add),
                        reads=[pool[4], S_f32], writes=[S_f32])

    load_w(wo_ap, False)
    for tt in range(NT):
        g0 = tt * TTL
        for o in range(NCH):
            pb = pool[o % 2]
            for k in range(NCH):
                p.op("tensor", lambda t, k=k, o=o, pb=pb, g0=g0: t.matmul(
                    pb[:, 0:TTL], wbuf[k][:, o * 128:(o + 1) * 128],
                    vals[k][:, g0:g0 + TTL], start=(k == 0), stop=False),
                    reads=[wbuf[k], vals[k]], writes=[pb])
            p.op("tensor", lambda t, o=o, pb=pb: t.matmul(
                pb[:, 0:TTL], brow["bo"][0:1, o * 128:(o + 1) * 128],
                c["ones_row_bf4"][0:1, 0:TTL], start=False, stop=True),
                reads=[brow["bo"], c["ones_row_bf4"]], writes=[pb])
            p.dma("sync", xt[o][:], x_ap[o * 128:(o + 1) * 128, g0:g0 + TTL],
                  writes=[xt[o]])
            p.op("vector", lambda v, o=o, pb=pb: v.tensor_tensor(
                xt[o][:], xt[o][:], pb[:, 0:TTL], op=ALU.add),
                reads=[pb, xt[o]], writes=[xt[o]])
            store_deps.append(p.dma(
                "scalar", y_ap[o * 128:(o + 1) * 128, g0:g0 + TTL],
                xt[o][:], reads=[xt[o]]))

    return store_deps[-NCH:]


def build_l0_attn(TTL=512):
    nc = bass.Bass("TRN2", target_bir_lowering=False, debug=False)
    p = Prog(nc)
    H = 16
    C = 128
    NC = S // C
    NT = S // TTL
    x_ap = nc.dram_tensor("x", [D, S], F32, kind="ExternalInput").ap()
    wq_ap = nc.dram_tensor("Wq", [D, D], F32, kind="ExternalInput").ap()
    wk_ap = nc.dram_tensor("Wk", [D, D], F32, kind="ExternalInput").ap()
    wv_ap = nc.dram_tensor("Wv", [D, D], F32, kind="ExternalInput").ap()
    wo_ap = nc.dram_tensor("Wo", [D, D], F32, kind="ExternalInput").ap()
    bq_ap = nc.dram_tensor("bq", [1, D], F32, kind="ExternalInput").ap()
    bk_ap = nc.dram_tensor("bk", [1, D], F32, kind="ExternalInput").ap()
    bv_ap = nc.dram_tensor("bv", [1, D], F32, kind="ExternalInput").ap()
    bo_ap = nc.dram_tensor("bo", [1, D], F32, kind="ExternalInput").ap()
    y_ap = nc.dram_tensor("y", [D, S], F32, kind="ExternalOutput").ap()

    c = emit_consts(p)
    sd = emit_l0_attn(p, c, {"x": x_ap, "Wq": wq_ap, "Wk": wk_ap, "Wv": wv_ap, "Wo": wo_ap, "bq": bq_ap, "bk": bk_ap, "bv": bv_ap, "bo": bo_ap, "y": y_ap}, TTL=TTL)
    return p.build(sd), ["x", "Wq", "Wk", "Wv", "Wo",
                                        "bq", "bk", "bv", "bo"]


def emit_l2_attn(p, c, aps, TTL=512):
    x_ap = aps["x"]
    wqc_ap = aps["Wqc"]
    wkc_ap = aps["Wkc"]
    wv_ap = aps["Wv"]
    wo_ap = aps["Wo"]
    y_ap = aps["y"]
    H = 16
    C = 128
    NC = S // C
    NT = S // TTL
    pool = [p.psum(f"pp{i}") for i in range(6)]

    wbuf = [p.sbuf(f"wbuf_{k}", [128, D], BF16) for k in range(NCH)]
    wcs = p.sbuf("wcs", [1, D], BF16)

    def load_w(ap, colsum):
        for k in range(NCH):
            p.dma("gpsimd", wbuf[k][:], ap[k * 128:(k + 1) * 128, :],
                  writes=[wbuf[k]])
        if colsum:
            for s_ in range(D // 512):
                c0, c1 = s_ * 512, (s_ + 1) * 512
                for k in range(NCH):
                    p.op("tensor", lambda t, k=k, c0=c0, c1=c1: t.matmul(
                        pool[3][0:1, 0:512], c["ones_col_bf"][:],
                        wbuf[k][:, c0:c1], start=(k == 0), stop=(k == NCH - 1)),
                        reads=[wbuf[k], c["ones_col_bf"]], writes=[pool[3]])
                p.op("vector", lambda v, c0=c0, c1=c1: v.tensor_copy(
                    wcs[0:1, c0:c1], pool[3][0:1, 0:512]),
                    reads=[pool[3]], writes=[wcs])

    v_tm = [p.sbuf(f"vtm_{i}", [128, 16 * 65], BF16) for i in range(NC)]
    ln_bufs = alloc_ln_bufs(p, tt_len=TTL, xs_width=S, xb=v_tm[0:NCH])
    xs = ln_bufs["xs"]
    negrm = ln_bufs["negrm"]
    xt = [p.sbuf(f"xt_{k}", [128, TTL], F32) for k in range(NCH)]
    vals = [p.sbuf(f"vals_{k}", [128, S], BF16) for k in range(NCH)]
    qh = p.sbuf("qh", [64, S], BF16)
    kh = p.sbuf("kh", [64, S], BF16)
    wsm_q = [p.sbuf(f"wsmq_{k}", [128, 128], BF16) for k in range(NCH)]
    wsm_k = [p.sbuf(f"wsmk_{k}", [128, 128], BF16) for k in range(NCH)]
    csq = p.sbuf("csq", [1, 128], BF16)
    csk = p.sbuf("csk", [1, 128], BF16)
    cosT = p.sbuf("cosT", [64, S], BF16)
    sinS = p.sbuf("sinS", [64, S], BF16)
    et = [p.sbuf(f"et_{i}", [128, TTL], BF16) for i in range(NC)]
    dmask = [p.sbuf(f"dmask_{j}", [128, TTL], BF16) for j in range(4)]
    zrow = p.sbuf("zrow", [1, 2 * TTL], F32)
    ovf = p.sbuf("ovf", [64, TTL], F32)
    rsc = p.sbuf("rsc", [1, S + 64], F32)
    rcol = p.sbuf("rcol", [64, 4], F32)
    rth = p.sbuf("rth", [64, TTL], F32)
    store_deps = []

    pos = rsc[0:1, 0:S]
    p.op("vector", lambda v: v.memset(rsc[0:1, 0:S], 1.0), writes=[rsc])
    p.op("vector", lambda v: v.tensor_tensor_scan(
        pos, rsc[0:1, 0:S], rsc[0:1, 0:S], 0.0, op0=ALU.mult, op1=ALU.add),
        reads=[rsc], writes=[rsc])
    p.op("vector", lambda v: v.tensor_scalar_add(pos, pos, -1.0),
         reads=[rsc], writes=[rsc])
    dd = rcol[0:64, 0:1]
    fc = rcol[0:64, 1:2]
    sg = rcol[0:64, 2:3]
    p.op("vector", lambda v: v.tensor_scalar(
        dd, c["rowid"][0:64, :], 0.5, -0.25, op0=ALU.mult,
        op1=ALU.add), reads=[c["rowid"]], writes=[rcol])
    p.op("vector", lambda v: v.tensor_scalar(
        dd, dd, MAGIC_RN, -MAGIC_RN, op0=ALU.add, op1=ALU.add),
        reads=[rcol], writes=[rcol])
    p.op("scalar", lambda s: s.activation(
        fc, dd, AF.Exp, scale=float(-2.0 * np.log(10000.0) / 64.0)),
        reads=[rcol], writes=[rcol])
    p.op("vector", lambda v: v.tensor_scalar(
        sg, dd, -2.0, None, op0=ALU.mult), reads=[rcol], writes=[rcol])
    p.op("vector", lambda v: v.tensor_tensor(
        sg, sg, c["rowid"][0:64, :], op=ALU.add),
        reads=[rcol, c["rowid"]], writes=[rcol])
    p.op("vector", lambda v: v.tensor_scalar(
        sg, sg, 2.0, -1.0, op0=ALU.mult, op1=ALU.add),
        reads=[rcol], writes=[rcol])
    p.op("tensor", lambda t: t.transpose(
        pool[0][0:1, 0:64], fc, c["ident_f"][0:64, 0:64]),
        reads=[rcol, c["ident_f"]], writes=[pool[0]])
    p.op("vector", lambda v: v.tensor_copy(rsc[0:1, S:S + 64], pool[0][0:1, 0:64]),
         reads=[pool[0]], writes=[rsc])
    for qt in range(NT):
        t0 = qt * TTL
        p.op("tensor", lambda t, t0=t0: t.matmul(
            pool[1][0:64, 0:TTL],
            rsc[0:1, S:S + 64], pos[0:1, t0:t0 + TTL],
            start=True, stop=True), reads=[rsc], writes=[pool[1]])
        p.op("vector", lambda v: v.tensor_scalar(
            rth[0:64, 0:TTL], pool[1][0:64, 0:TTL],
            float(1.0 / (2 * np.pi)), MAGIC_RN, op0=ALU.mult, op1=ALU.add),
            reads=[pool[1]], writes=[rth])
        p.op("vector", lambda v: v.tensor_scalar(
            rth[0:64, 0:TTL], rth[0:64, 0:TTL], -MAGIC_RN,
            float(-2 * np.pi), op0=ALU.add, op1=ALU.mult),
            reads=[rth], writes=[rth])
        p.op("vector", lambda v: v.tensor_tensor(
            rth[0:64, 0:TTL], rth[0:64, 0:TTL], pool[1][0:64, 0:TTL],
            op=ALU.add), reads=[pool[1], rth], writes=[rth])
        p.op("vector", lambda v: v.tensor_scalar(
            rth[0:64, 0:TTL], rth[0:64, 0:TTL], float(np.pi),
            float(-np.pi), op0=ALU.min, op1=ALU.max),
            reads=[rth], writes=[rth])
        p.op("scalar", lambda s, t0=t0: s.activation(
            ovf[0:64, 0:TTL], rth[0:64, 0:TTL], AF.Sin),
            reads=[rth], writes=[ovf])
        p.op("vector", lambda v, t0=t0: v.tensor_scalar(
            sinS[0:64, t0:t0 + TTL], ovf[0:64, 0:TTL], sg, None,
            op0=ALU.mult), reads=[ovf, rcol], writes=[sinS])
        p.op("scalar", lambda s: s.activation(
            ovf[0:64, 0:TTL], rth[0:64, 0:TTL], AF.Sin, scale=0.5),
            reads=[rth], writes=[ovf])
        p.op("scalar", lambda s: s.activation(
            ovf[0:64, 0:TTL], ovf[0:64, 0:TTL], AF.Square),
            reads=[ovf], writes=[ovf])
        p.op("vector", lambda v, t0=t0: v.tensor_scalar(
            cosT[0:64, t0:t0 + TTL], ovf[0:64, 0:TTL], -2.0, 1.0,
            op0=ALU.mult, op1=ALU.add), reads=[ovf], writes=[cosT])

    for tt in range(NT):
        g0 = tt * TTL
        for k in range(NCH):
            p.dma("sync", xt[k][:], x_ap[k * 128:(k + 1) * 128, g0:g0 + TTL],
                  writes=[xt[k]])
        emit_ln(p, c, xt, ln_bufs, pool[4], pool[5], tt_len=TTL, off=g0)

    p.op("tensor", lambda t: t.matmul(
        pool[0][:, 0:TTL], c["ones_full_f"][0:1, :], pos[0:1, 0:TTL],
        start=True, stop=True), reads=[rsc, c["ones_full_f"]],
        writes=[pool[0]])
    for j in range(4):
        p.op("vector", lambda v, j=j: v.tensor_scalar(
            dmask[j][:], pool[0][:, 0:TTL], float(-j * 128), None,
            op0=ALU.add), reads=[pool[0]], writes=[dmask[j]])
        p.op("vector", lambda v, j=j: v.tensor_tensor(
            dmask[j][:], dmask[j][:],
            c["rowid"][:].to_broadcast([128, TTL]), op=ALU.is_ge),
            reads=[dmask[j], c["rowid"]], writes=[dmask[j]])

    load_w(wv_ap, True)
    for i in range(NC):
        t0 = i * C
        for vs in range(2):
            c0, c1 = vs * 512, (vs + 1) * 512
            pb = pool[vs]
            for k in range(NCH):
                p.op("tensor", lambda t, k=k, c0=c0, c1=c1, pb=pb, t0=t0:
                     t.matmul(pb[:, 0:512], xs[k][:, t0:t0 + C],
                              wbuf[k][:, c0:c1], start=(k == 0), stop=False),
                     reads=[xs[k], wbuf[k]], writes=[pb])
            p.op("tensor", lambda t, c0=c0, c1=c1, pb=pb, t0=t0: t.matmul(
                pb[:, 0:512], negrm[0:1, t0:t0 + C], wcs[0:1, c0:c1],
                start=False, stop=True),
                reads=[negrm, wcs], writes=[pb])
            p.op("vector", lambda v, i=i, vs=vs, pb=pb: v.tensor_copy(
                v_tm[i][:].rearrange("p (h e) -> p h e", e=65)
                [:, vs * 8:(vs + 1) * 8, 0:64], pb[:, 0:512].rearrange(
                    "p (h e) -> p h e", e=64)), reads=[pb],
                writes=[v_tm[i]])

    for i in range(NC):
        p.op("vector", lambda v, i=i: v.memset(
            v_tm[i][:].rearrange("p (h e) -> p h e", e=65)[:, :, 64:65], 1.0),
            writes=[v_tm[i]])

    for h in range(H):
        for k in range(NCH):
            p.dma("gpsimd", wsm_q[k][:], wqc_ap[h, k * 128:(k + 1) * 128, :],
                  writes=[wsm_q[k]])
            p.dma("gpsimd", wsm_k[k][:], wkc_ap[h, k * 128:(k + 1) * 128, :],
                  writes=[wsm_k[k]])
        for (wsm, cs) in ((wsm_q, csq), (wsm_k, csk)):
            for k in range(NCH):
                p.op("tensor", lambda t, k=k, wsm=wsm: t.matmul(
                    pool[3][0:1, 0:128], c["ones_col_bf"][:], wsm[k][:],
                    start=(k == 0), stop=(k == NCH - 1)),
                    reads=[wsm[k], c["ones_col_bf"]], writes=[pool[3]])
            p.op("vector", lambda v, cs=cs: v.tensor_copy(
                cs[0:1, :], pool[3][0:1, 0:128]), reads=[pool[3]], writes=[cs])
        for tt in range(NT):
            t0 = tt * TTL
            for (wsm, cs, dst) in ((wsm_q, csq, qh), (wsm_k, csk, kh)):
                pb = pool[4]
                for k in range(NCH):
                    p.op("tensor", lambda t, k=k, wsm=wsm, t0=t0, pb=pb:
                         t.matmul(pb[:, 0:TTL], wsm[k][:],
                                  xs[k][:, t0:t0 + TTL], start=(k == 0),
                                  stop=False),
                         reads=[xs[k], wsm[k]], writes=[pb])
                p.op("tensor", lambda t, cs=cs, t0=t0, pb=pb: t.matmul(
                    pb[:, 0:TTL], cs[0:1, :], negrm[0:1, t0:t0 + TTL],
                    start=False, stop=True), reads=[cs, negrm], writes=[pb])
                p.op("vector", lambda v, t0=t0, pb=pb: v.tensor_tensor(
                    ovf[0:64, 0:TTL], pb[0:64, 0:TTL], cosT[0:64, t0:t0 + TTL],
                    op=ALU.mult), reads=[pb, cosT], writes=[ovf])
                p.op("vector", lambda v, t0=t0, pb=pb, dst=dst: v.tensor_tensor(
                    dst[0:64, t0:t0 + TTL], pb[64:128, 0:TTL],
                    sinS[0:64, t0:t0 + TTL], op=ALU.mult),
                    reads=[pb, sinS], writes=[dst])
                p.op("vector", lambda v, t0=t0, dst=dst: v.tensor_tensor(
                    dst[0:64, t0:t0 + TTL], dst[0:64, t0:t0 + TTL],
                    ovf[0:64, 0:TTL], op=ALU.add),
                    reads=[dst, ovf], writes=[dst])
        for qt in range(NT):
            q0 = qt * TTL
            nst = (qt + 1) * (TTL // C)
            for st in range(nst):
                s0 = st * C
                p.op("tensor", lambda t, s0=s0, q0=q0: t.matmul(
                    pool[0][:, 0:TTL], kh[0:64, s0:s0 + C],
                    qh[0:64, q0:q0 + TTL], start=True, stop=True),
                    reads=[kh, qh], writes=[pool[0]])
                p.op("scalar", lambda s, st=st: s.activation(
                    et[st][:], pool[0][:, 0:TTL], AF.Exp, scale=0.125),
                    reads=[pool[0]], writes=[et[st]])
                j = st - qt * (TTL // C)
                if j >= 0:
                    p.op("vector", lambda v, st=st, j=j: v.tensor_tensor(
                        et[st][:], et[st][:], dmask[j][:], op=ALU.mult),
                        reads=[et[st], dmask[j]], writes=[et[st]])
            for st in range(nst):
                p.op("tensor", lambda t, st=st, h=h: t.matmul(
                    pool[1][0:64, 0:TTL], v_tm[st][:, h * 65:h * 65 + 64],
                    et[st][:], start=(st == 0), stop=(st == nst - 1)),
                    reads=[v_tm[st], et[st]], writes=[pool[1]])
            for st in range(nst):
                p.op("tensor", lambda t, st=st, h=h: t.matmul(
                    pool[2][0:1, 0:TTL],
                    v_tm[st][:, h * 65 + 64:h * 65 + 65],
                    et[st][:], start=(st == 0), stop=(st == nst - 1),
                    skip_group_check=True),
                    reads=[v_tm[st], et[st]], writes=[pool[2]])
            p.op("vector", lambda v: v.tensor_copy(
                zrow[0:1, 0:TTL], pool[2][0:1, 0:TTL]),
                reads=[pool[2]], writes=[zrow])
            p.op("vector", lambda v: v.reciprocal(
                zrow[0:1, TTL:2 * TTL], zrow[0:1, 0:TTL]),
                reads=[zrow], writes=[zrow])
            p.op("tensor", lambda t: t.matmul(
                pool[3][0:64, 0:TTL], c["ones_full_f"][0:1, 0:64],
                zrow[0:1, TTL:2 * TTL], start=True, stop=True),
                reads=[zrow, c["ones_full_f"]], writes=[pool[3]])
            p.op("vector", lambda v: v.tensor_copy(
                ovf[0:64, 0:TTL], pool[1][0:64, 0:TTL]),
                reads=[pool[1]], writes=[ovf])
            p.op("vector", lambda v, h=h, q0=q0: v.tensor_tensor(
                vals[h // 2][64 * (h % 2):64 * (h % 2) + 64, q0:q0 + TTL],
                ovf[0:64, 0:TTL], pool[3][0:64, 0:TTL], op=ALU.mult),
                reads=[ovf, pool[3]], writes=[vals[h // 2]])

    load_w(wo_ap, False)
    for tt in range(NT):
        g0 = tt * TTL
        for o in range(NCH):
            pb = pool[o % 2]
            for k in range(NCH):
                p.op("tensor", lambda t, k=k, o=o, pb=pb, g0=g0: t.matmul(
                    pb[:, 0:TTL], wbuf[k][:, o * 128:(o + 1) * 128],
                    vals[k][:, g0:g0 + TTL], start=(k == 0),
                    stop=(k == NCH - 1)),
                    reads=[wbuf[k], vals[k]], writes=[pb])
            p.dma("sync", xt[o][:], x_ap[o * 128:(o + 1) * 128, g0:g0 + TTL],
                  writes=[xt[o]])
            p.op("vector", lambda v, o=o, pb=pb: v.tensor_tensor(
                xt[o][:], xt[o][:], pb[:, 0:TTL], op=ALU.add),
                reads=[pb, xt[o]], writes=[xt[o]])
            store_deps.append(p.dma(
                "scalar", y_ap[o * 128:(o + 1) * 128, g0:g0 + TTL],
                xt[o][:], reads=[xt[o]]))

    return store_deps[-NCH:]


def build_l2_attn(TTL=512):
    nc = bass.Bass("TRN2", target_bir_lowering=False, debug=False)
    p = Prog(nc)
    H = 16
    C = 128
    NC = S // C
    NT = S // TTL
    x_ap = nc.dram_tensor("x", [D, S], F32, kind="ExternalInput").ap()
    wqc_ap = nc.dram_tensor("Wqc", [H, D, 128], F32, kind="ExternalInput").ap()
    wkc_ap = nc.dram_tensor("Wkc", [H, D, 128], F32, kind="ExternalInput").ap()
    wv_ap = nc.dram_tensor("Wv", [D, D], F32, kind="ExternalInput").ap()
    wo_ap = nc.dram_tensor("Wo", [D, D], F32, kind="ExternalInput").ap()
    y_ap = nc.dram_tensor("y", [D, S], F32, kind="ExternalOutput").ap()

    c = emit_consts(p)
    sd = emit_l2_attn(p, c, {"x": x_ap, "Wqc": wqc_ap, "Wkc": wkc_ap, "Wv": wv_ap, "Wo": wo_ap, "y": y_ap}, TTL=TTL)
    return p.build(sd), ["x", "Wqc", "Wkc", "Wv", "Wo"]


def emit_l1_attn(p, c, aps, TTL=512):
    x_ap = aps["x"]
    win_ap = aps["Win"]
    wr_ap = aps["Wr"]
    wk_ap = aps["Wk"]
    wv_ap = aps["Wv"]
    wb_ap = aps["Wb"]
    wq_ap = aps["Wq"]
    wo_ap = aps["WoT"]
    bb_ap = aps["bbeta"]
    y_ap = aps["y"]
    C = 128
    NC = S // C
    NT = S // TTL
    pool = [p.psum(f"pp{i}") for i in range(6)]

    win = [p.sbuf(f"win_{k}", [128, D], BF16) for k in range(NCH)]
    for k in range(NCH):
        p.dma("gpsimd", win[k][:], win_ap[k * 128:(k + 1) * 128, :],
              writes=[win[k]])
    wsm = {}
    for nm, ap, cols in (("wr", wr_ap, 4), ("wk", wk_ap, 128),
                         ("wv", wv_ap, 128), ("wb", wb_ap, 128),
                         ("wq", wq_ap, 32)):
        wsm[nm] = [p.sbuf(f"{nm}_{k}", [128, cols], BF16) for k in range(NCH)]
        for k in range(NCH):
            p.dma("gpsimd", wsm[nm][k][:], ap[k * 128:(k + 1) * 128, :],
                  writes=[wsm[nm][k]])
    wout = p.sbuf("wout", [32, D], BF16)
    p.dma("gpsimd", wout[:], wo_ap, writes=[wout])
    bbet = p.sbuf("bbet", [128, 1], F32)
    p.dma("sync", bbet[:], bb_ap, writes=[bbet])

    ln_bufs = alloc_ln_bufs(p, tt_len=TTL, xs_width=S)
    xs = ln_bufs["xs"]
    negrm = ln_bufs["negrm"]
    xt = [p.sbuf(f"xt_{k}", [128, TTL], F32) for k in range(NCH)]
    xp = [p.sbuf(f"xp_{k}", [128, S], BF16) for k in range(NCH)]
    kf = p.sbuf("kf", [128, S], BF16)
    kn = p.sbuf("kn", [128, S], BF16)
    vf = p.sbuf("vf", [128, S], BF16)
    btf = p.sbuf("btf", [128, S], F32)
    qp = p.sbuf("qp", [32, S], BF16)
    uw_fm = p.sbuf("uw_fm", [4, S], BF16)
    sm_fm = p.sbuf("sm_fm", [4, S], BF16)
    bp4b = p.sbuf("bp4b", [4, 128], BF16)
    bo_all = p.sbuf("bo_all", [128, S], BF16)
    T_all = p.sbuf("T_all", [128, 32 * C], BF16)
    T_q = p.sbuf("T_q", [128, 32 * C], BF16)
    rhs_t = p.sbuf("rhs_t", [128, 32 * C], BF16)
    uwb = p.sbuf("uwb", [128, 2 * C], F32)
    smb = p.sbuf("smb", [128, TTL], F32)
    Sa = p.sbuf("Sa", [128, 32], F32)
    Sb = p.sbuf("Sb", [128, 32], F32)
    s32 = p.sbuf("s32", [128, 4 * 32], F32)
    scol = p.sbuf("scol", [128, 8], F32)
    gsc = p.sbuf("gsc", [128, 24], F32)
    gmax = p.sbuf("gmax", [128, 8], F32)
    jsel = p.sbuf("jsel", [128, 32], F32)
    bp4c = p.sbuf("bp4c", [128, 4], F32)
    bp4f = p.sbuf("bp4f", [4, 128], F32)
    nrm = p.sbuf("nrm", [4, 512], F32)
    obf32 = p.sbuf("obf32", [32, 128], BF16)
    bpnn = p.sbuf("bpnn", [128, 128], BF16)
    rowsel = p.sbuf("rowsel", [128, 32], BF16)
    blkid = p.sbuf("blkid", [128, 2], F32)
    eye_dram = p.dram("eye_dram", [128, 32], F32)
    store_deps = []

    bid = blkid[:, 0:1]
    imod = blkid[:, 1:2]
    p.op("vector", lambda v: v.tensor_scalar(
        bid, c["rowid"][:], 1.0 / 32.0, -15.5 / 32.0, op0=ALU.mult,
        op1=ALU.add), reads=[c["rowid"]], writes=[blkid])
    p.op("vector", lambda v: v.tensor_scalar(
        bid, bid, MAGIC_RN, -MAGIC_RN, op0=ALU.add, op1=ALU.add),
        reads=[blkid], writes=[blkid])
    p.op("vector", lambda v: v.tensor_scalar(
        imod, bid, -32.0, None, op0=ALU.mult), reads=[blkid], writes=[blkid])
    p.op("vector", lambda v: v.tensor_tensor(
        imod, imod, c["rowid"][:], op=ALU.add),
        reads=[blkid, c["rowid"]], writes=[blkid])
    p.op("vector", lambda v: v.tensor_tensor(
        rowsel[:], c["colramp"][:, 0:32], imod.to_broadcast([128, 32]),
        op=ALU.is_equal), reads=[c["colramp"], blkid], writes=[rowsel])
    for n in range(4):
        p.op("vector", lambda v, n=n: v.tensor_scalar(
            bp4c[:, n:n + 1], bid, float(n), None, op0=ALU.is_equal),
            reads=[blkid], writes=[bp4c])
    p.op("tensor", lambda t: t.transpose(
        pool[0][0:4, 0:128], bp4c[:], c["ident_f"][:]),
        reads=[bp4c, c["ident_f"]], writes=[pool[0]])
    p.op("vector", lambda v: v.tensor_copy(bp4f[0:4, :], pool[0][0:4, 0:128]),
         reads=[pool[0]], writes=[bp4f])
    p.op("vector", lambda v: v.memset(obf32[:], 1.0), writes=[obf32])
    p.op("vector", lambda v: v.tensor_copy(bp4b[0:4, :], bp4f[0:4, :]),
         reads=[bp4f], writes=[bp4b])
    p.op("vector", lambda v: v.tensor_scalar(
        s32[:, 0:128], c["colramp"][:], 1.0 / 32.0, -15.5 / 32.0,
        op0=ALU.mult, op1=ALU.add), reads=[c["colramp"]], writes=[s32])
    p.op("vector", lambda v: v.tensor_scalar(
        s32[:, 0:128], s32[:, 0:128], MAGIC_RN, -MAGIC_RN, op0=ALU.add,
        op1=ALU.add), reads=[s32], writes=[s32])
    p.op("vector", lambda v: v.tensor_tensor(
        bpnn[:], s32[:, 0:128], bid.to_broadcast([128, 128]),
        op=ALU.is_equal), reads=[s32, blkid], writes=[bpnn])
    for rr in range(4):
        p.dma("scalar", eye_dram[rr * 32:(rr + 1) * 32, :],
              c["ident_f"][0:32, 0:32], reads=[c["ident_f"]],
              writes=[eye_dram])
    p.dma("sync", jsel[:], eye_dram, reads=[eye_dram], writes=[jsel])

    for tt in range(NT):
        g0 = tt * TTL
        for k in range(NCH):
            p.dma("sync", xt[k][:], x_ap[k * 128:(k + 1) * 128, g0:g0 + TTL],
                  writes=[xt[k]])
        emit_ln(p, c, xt, ln_bufs, pool[4], pool[5], tt_len=TTL, off=g0)
    wdict = {"nch": NCH, "tiles": win, "csw": None}
    wcs = p.sbuf("wcs", [1, D], BF16)
    for s_ in range(D // 512):
        c0, c1 = s_ * 512, (s_ + 1) * 512
        for k in range(NCH):
            p.op("tensor", lambda t, k=k, c0=c0, c1=c1: t.matmul(
                pool[3][0:1, 0:512], c["ones_col_bf"][:], win[k][:, c0:c1],
                start=(k == 0), stop=(k == NCH - 1)),
                reads=[win[k], c["ones_col_bf"]], writes=[pool[3]])
        p.op("vector", lambda v, c0=c0, c1=c1: v.tensor_copy(
            wcs[0:1, c0:c1], pool[3][0:1, 0:512]),
            reads=[pool[3]], writes=[wcs])
    wdict["csw"] = wcs
    for tt in range(NT):
        g0 = tt * TTL
        for o in range(NCH):
            pb = pool[o % 2]
            emit_linear_acc(p, wdict, xs, negrm, pb, o * 128, (o + 1) * 128,
                            tt_len=TTL, xs_sl=(g0, g0 + TTL),
                            nm_sl=(g0, g0 + TTL))
            p.op("scalar", lambda sce, o=o, pb=pb, g0=g0: sce.activation(
                xp[o][:, g0:g0 + TTL], pb[:, 0:TTL], AF.Silu),
                reads=[pb], writes=[xp[o]])

    def proj(dst, wname, ncols, evac):
        for tt in range(NT):
            g0 = tt * TTL
            pb = pool[2]
            for k in range(NCH):
                p.op("tensor", lambda t, k=k, g0=g0, pb=pb, wname=wname,
                     ncols=ncols: t.matmul(
                         pb[0:ncols, 0:TTL], wsm[wname][k][:, 0:ncols],
                         xp[k][:, g0:g0 + TTL], start=(k == 0),
                         stop=(k == NCH - 1)),
                     reads=[xp[k], wsm[wname][k]], writes=[pb])
            evac(pb, g0)
    proj(kf, "wk", 128, lambda pb, g0: p.op(
        "vector", lambda v: v.tensor_copy(kf[:, g0:g0 + TTL], pb[:, 0:TTL]),
        reads=[pb], writes=[kf]))
    proj(vf, "wv", 128, lambda pb, g0: p.op(
        "vector", lambda v: v.tensor_copy(vf[:, g0:g0 + TTL], pb[:, 0:TTL]),
        reads=[pb], writes=[vf]))
    proj(btf, "wb", 128, lambda pb, g0: p.op(
        "scalar", lambda sce: sce.activation(
            btf[:, g0:g0 + TTL], pb[:, 0:TTL], AF.Sigmoid, bias=bbet[:]),
        reads=[pb, bbet], writes=[btf]))
    proj(qp, "wq", 32, lambda pb, g0: p.op(
        "vector", lambda v: v.tensor_copy(qp[0:32, g0:g0 + TTL],
                                          pb[0:32, 0:TTL]),
        reads=[pb], writes=[qp]))

    for sl in range(S // 128):
        t0 = sl * 128
        raw = gsc[:, 0:4]
        lg = gsc[:, 8:16]
        wv_ = gsc[:, 16:20]
        for k in range(NCH):
            p.op("tensor", lambda t, k=k, t0=t0: t.matmul(
                pool[3][:, 0:4], xp[k][:, t0:t0 + 128], wsm["wr"][k][:, 0:4],
                start=(k == 0), stop=(k == NCH - 1)),
                reads=[xp[k], wsm["wr"][k]], writes=[pool[3]])
        p.op("vector", lambda v: v.tensor_copy(raw, pool[3][:, 0:4]),
             reads=[pool[3]], writes=[gsc])
        p.op("vector", lambda v: v.reduce_max(
            gmax[:, 0:1], raw, axis=mybir.AxisListType.X, negate=True),
            reads=[gsc], writes=[gmax])
        p.op("scalar", lambda sce: sce.activation(
            gsc[:, 4:8], raw, AF.Exp, bias=gmax[:, 0:1],
            accum_out=gmax[:, 2:3]), reads=[gsc, gmax], writes=[gsc, gmax])
        p.op("vector", lambda v: v.reciprocal(gmax[:, 3:4], gmax[:, 2:3]),
             reads=[gmax], writes=[gmax])
        p.op("vector", lambda v: v.tensor_scalar(
            gsc[:, 4:8], gsc[:, 4:8], gmax[:, 3:4], None, op0=ALU.mult),
            reads=[gsc, gmax], writes=[gsc])
        p.op("vector", lambda v: v.memset(lg, -1e30), writes=[gsc])
        p.op("vector", lambda v: v.tensor_copy(lg[:, 0:4], gsc[:, 4:8]),
             reads=[gsc], writes=[gsc])
        p.op("vector", lambda v: v.max(out=gmax[:], in_=lg),
             reads=[gsc], writes=[gmax])
        p.op("vector", lambda v: v.tensor_scalar(
            wv_, gsc[:, 4:8], gmax[:, 1:2], None, op0=ALU.is_ge),
            reads=[gsc, gmax], writes=[gsc])
        p.op("vector", lambda v: v.tensor_tensor(
            wv_, wv_, gsc[:, 4:8], op=ALU.mult), reads=[gsc], writes=[gsc])
        p.op("vector", lambda v: v.reduce_sum(
            gmax[:, 4:5], wv_, axis=mybir.AxisListType.X),
            reads=[gsc], writes=[gmax])
        p.op("vector", lambda v: v.tensor_scalar_add(
            gmax[:, 4:5], gmax[:, 4:5], 1e-8), reads=[gmax], writes=[gmax])
        p.op("vector", lambda v: v.reciprocal(gmax[:, 5:6], gmax[:, 4:5]),
             reads=[gmax], writes=[gmax])
        p.op("vector", lambda v: v.tensor_scalar(
            wv_, wv_, gmax[:, 5:6], None, op0=ALU.mult),
            reads=[gsc, gmax], writes=[gsc])
        p.op("tensor", lambda t: t.transpose(
            pool[3][0:4, 0:128], wv_, c["ident_f"][:]),
            reads=[gsc, c["ident_f"]], writes=[pool[3]])
        p.op("vector", lambda v, t0=t0: v.tensor_copy(
            uw_fm[0:4, t0:t0 + 128], pool[3][0:4, 0:128]),
            reads=[pool[3]], writes=[uw_fm])
        p.op("tensor", lambda t: t.transpose(
            pool[3][0:4, 0:128], gsc[:, 4:8], c["ident_f"][:]),
            reads=[gsc, c["ident_f"]], writes=[pool[3]])
        p.op("vector", lambda v, t0=t0: v.tensor_copy(
            sm_fm[0:4, t0:t0 + 128], pool[3][0:4, 0:128]),
            reads=[pool[3]], writes=[sm_fm])

    KW = 256
    for g0 in range(0, S, KW):
        p.op("scalar", lambda sce, g0=g0: sce.activation(
            smb[:, 0:KW], kf[:, g0:g0 + KW], AF.Square),
            reads=[kf], writes=[smb])
        p.op("tensor", lambda t, g0=g0: t.matmul(
            pool[3][0:4, 0:KW], bp4c[:], smb[:, 0:KW],
            start=True, stop=True),
            reads=[smb, bp4c], writes=[pool[3]])
        p.op("scalar", lambda sce: sce.activation(
            nrm[0:4, 0:KW], pool[3][0:4, 0:KW], AF.Sqrt),
            reads=[pool[3]], writes=[nrm])
        p.op("vector", lambda v: v.tensor_scalar_add(
            nrm[0:4, 0:KW], nrm[0:4, 0:KW], 1e-6), reads=[nrm],
            writes=[nrm])
        p.op("vector", lambda v: v.reciprocal(
            nrm[0:4, KW:2 * KW], nrm[0:4, 0:KW]), reads=[nrm],
            writes=[nrm])
        p.op("tensor", lambda t, g0=g0: t.matmul(
            pool[3][:, 0:KW], bp4f[0:4, :], nrm[0:4, KW:2 * KW],
            start=True, stop=True), reads=[nrm, bp4f], writes=[pool[3]])
        p.op("vector", lambda v, g0=g0: v.tensor_tensor(
            kn[:, g0:g0 + KW], kf[:, g0:g0 + KW], pool[3][:, 0:KW],
            op=ALU.mult), reads=[kf, pool[3]], writes=[kn])

    S_cur, S_nxt = Sa, Sb
    for ci in range(NC):
        t0 = ci * C
        p.op("vector", lambda v, t0=t0: v.tensor_tensor(
            rhs_t[:].rearrange("p (t j) -> p t j", j=32),
            kn[:, t0:t0 + C].rearrange("p (t o) -> p t o", o=1)
            .to_broadcast([128, C, 32]),
            jsel[:].rearrange("p (o j) -> p o j", o=1)
            .to_broadcast([128, C, 32]), op=ALU.mult),
            reads=[kn, jsel], writes=[rhs_t])
        for s_ in range(8):
            p.op("tensor", lambda t, s_=s_: t.matmul(
                pool[3][:, 0:512], bpnn[:], rhs_t[:, s_ * 512:(s_ + 1) * 512],
                start=True, stop=True), reads=[bpnn, rhs_t], writes=[pool[3]])
            p.op("vector", lambda v, s_=s_: v.tensor_copy(
                T_all[:, s_ * 512:(s_ + 1) * 512], pool[3][:, 0:512]),
                reads=[pool[3]], writes=[T_all])
        p.op("vector", lambda v, t0=t0: v.tensor_tensor(
            rhs_t[0:32, :].rearrange("p (t j) -> p t j", j=32),
            qp[0:32, t0:t0 + C].rearrange("p (t o) -> p t o", o=1)
            .to_broadcast([32, C, 32]),
            jsel[0:32, :].rearrange("p (o j) -> p o j", o=1)
            .to_broadcast([32, C, 32]), op=ALU.mult),
            reads=[qp, jsel], writes=[rhs_t])
        for s_ in range(8):
            p.op("tensor", lambda t, s_=s_: t.matmul(
                pool[3][:, 0:512], obf32[:],
                rhs_t[0:32, s_ * 512:(s_ + 1) * 512],
                start=True, stop=True),
                reads=[obf32, rhs_t], writes=[pool[3]])
            p.op("vector", lambda v, s_=s_: v.tensor_copy(
                T_q[:, s_ * 512:(s_ + 1) * 512], pool[3][:, 0:512]),
                reads=[pool[3]], writes=[T_q])
        p.op("tensor", lambda t, t0=t0: t.matmul(
            pool[3][:, 0:C], bp4b[0:4, :], uw_fm[0:4, t0:t0 + C],
            start=True, stop=True), reads=[uw_fm, bp4b], writes=[pool[3]])
        p.op("vector", lambda v: v.tensor_copy(uwb[:, 0:C], pool[3][:, 0:C]),
             reads=[pool[3]], writes=[uwb])
        p.op("vector", lambda v: v.tensor_scalar(
            uwb[:, C:2 * C], uwb[:, 0:C], -1.0, 1.0, op0=ALU.mult,
            op1=ALU.add), reads=[uwb], writes=[uwb])
        for tau in range(C):
            tg = t0 + tau
            Tc = T_all[:, tau * 32:(tau + 1) * 32]
            Tqc = T_q[:, tau * 32:(tau + 1) * 32]
            if ci == 0 and tau == 0:
                p.op("vector", lambda v, S_cur=S_cur: v.memset(S_cur[:], 0.0),
                     writes=[S_cur])
            p.op("vector", lambda v, Tc=Tc, S_cur=S_cur: v.tensor_tensor(
                s32[:, 0:32], S_cur[:], Tc, op=ALU.mult),
                reads=[S_cur, T_all], writes=[s32])
            p.op("vector", lambda v: v.reduce_sum(
                scol[:, 0:1], s32[:, 0:32], axis=mybir.AxisListType.X),
                reads=[s32], writes=[scol])
            p.op("vector", lambda v, tg=tg: v.tensor_tensor(
                scol[:, 1:2], vf[:, tg:tg + 1], scol[:, 0:1],
                op=ALU.subtract), reads=[vf, scol], writes=[scol])
            p.op("vector", lambda v, tg=tg, S_cur=S_cur: v.tensor_scalar(
                s32[:, 32:64], S_cur[:], btf[:, tg:tg + 1], None,
                op0=ALU.mult), reads=[S_cur, btf], writes=[s32])
            p.op("vector", lambda v, Tc=Tc: v.scalar_tensor_tensor(
                s32[:, 32:64], Tc, scol[:, 1:2], s32[:, 32:64],
                op0=ALU.mult, op1=ALU.add),
                reads=[T_all, scol, s32], writes=[s32])
            p.op("scalar", lambda sce: sce.activation(
                s32[:, 64:96], s32[:, 32:64], AF.Tanh),
                reads=[s32], writes=[s32])
            p.op("vector", lambda v, tau=tau, S_cur=S_cur: v.tensor_scalar(
                s32[:, 96:128], S_cur[:], uwb[:, C + tau:C + tau + 1], None,
                op0=ALU.mult), reads=[S_cur, uwb], writes=[s32])
            p.op("vector", lambda v, tau=tau, S_nxt=S_nxt: v.scalar_tensor_tensor(
                S_nxt[:], s32[:, 64:96], uwb[:, tau:tau + 1],
                s32[:, 96:128], op0=ALU.mult, op1=ALU.add),
                reads=[s32, uwb], writes=[S_nxt])
            p.op("vector", lambda v, Tqc=Tqc, S_nxt=S_nxt: v.tensor_tensor(
                s32[:, 0:32], S_nxt[:], Tqc, op=ALU.mult),
                reads=[S_nxt, T_q], writes=[s32])
            p.op("vector", lambda v: v.reduce_sum(
                scol[:, 2:3], s32[:, 0:32], axis=mybir.AxisListType.X),
                reads=[s32], writes=[scol])
            p.op("scalar", lambda sce: sce.activation(
                scol[:, 3:4], scol[:, 2:3], AF.Sigmoid),
                reads=[scol], writes=[scol])
            p.op("vector", lambda v: v.tensor_tensor(
                scol[:, 4:5], scol[:, 2:3], scol[:, 3:4], op=ALU.mult),
                reads=[scol], writes=[scol])
            p.op("vector", lambda v, tg=tg: v.tensor_tensor(
                bo_all[:, tg:tg + 1], scol[:, 4:5], scol[:, 2:3],
                op=ALU.mult), reads=[scol], writes=[bo_all])
            S_cur, S_nxt = S_nxt, S_cur

    for tt in range(NT):
        g0 = tt * TTL
        p.op("tensor", lambda t, g0=g0: t.matmul(
            pool[3][:, 0:TTL], bp4b[0:4, :], sm_fm[0:4, g0:g0 + TTL],
            start=True, stop=True), reads=[sm_fm, bp4b], writes=[pool[3]])
        p.op("vector", lambda v, g0=g0: v.tensor_tensor(
            T_all[:, 0:TTL], bo_all[:, g0:g0 + TTL], pool[3][:, 0:TTL],
            op=ALU.mult), reads=[bo_all, pool[3]], writes=[T_all])
        p.op("tensor", lambda t: t.matmul(
            pool[2][0:32, 0:TTL], rowsel[:], T_all[:, 0:TTL],
            start=True, stop=True), reads=[rowsel, T_all], writes=[pool[2]])
        p.op("vector", lambda v: v.tensor_copy(
            T_q[0:32, 0:TTL], pool[2][0:32, 0:TTL]),
            reads=[pool[2]], writes=[T_q])
        for o in range(NCH):
            pb = pool[o % 2]
            p.op("tensor", lambda t, o=o, pb=pb: t.matmul(
                pb[:, 0:TTL], wout[:, o * 128:(o + 1) * 128],
                T_q[0:32, 0:TTL], start=True, stop=True),
                reads=[wout, T_q], writes=[pb])
            p.dma("sync", xt[o][:], x_ap[o * 128:(o + 1) * 128, g0:g0 + TTL],
                  writes=[xt[o]])
            p.op("vector", lambda v, o=o, pb=pb: v.tensor_tensor(
                xt[o][:], xt[o][:], pb[:, 0:TTL], op=ALU.add),
                reads=[pb, xt[o]], writes=[xt[o]])
            store_deps.append(p.dma(
                "scalar", y_ap[o * 128:(o + 1) * 128, g0:g0 + TTL],
                xt[o][:], reads=[xt[o]]))

    return store_deps[-NCH:]


def build_l1_attn(TTL=512):
    nc = bass.Bass("TRN2", target_bir_lowering=False, debug=False)
    p = Prog(nc)
    C = 128
    NC = S // C
    NT = S // TTL
    x_ap = nc.dram_tensor("x", [D, S], F32, kind="ExternalInput").ap()
    win_ap = nc.dram_tensor("Win", [D, D], F32, kind="ExternalInput").ap()
    wr_ap = nc.dram_tensor("Wr", [D, 4], F32, kind="ExternalInput").ap()
    wk_ap = nc.dram_tensor("Wk", [D, 128], F32, kind="ExternalInput").ap()
    wv_ap = nc.dram_tensor("Wv", [D, 128], F32, kind="ExternalInput").ap()
    wb_ap = nc.dram_tensor("Wb", [D, 128], F32, kind="ExternalInput").ap()
    wq_ap = nc.dram_tensor("Wq", [D, 32], F32, kind="ExternalInput").ap()
    wo_ap = nc.dram_tensor("WoT", [32, D], F32, kind="ExternalInput").ap()
    bb_ap = nc.dram_tensor("bbeta", [128, 1], F32, kind="ExternalInput").ap()
    y_ap = nc.dram_tensor("y", [D, S], F32, kind="ExternalOutput").ap()

    c = emit_consts(p)
    sd = emit_l1_attn(p, c, {"x": x_ap, "Win": win_ap, "Wr": wr_ap, "Wk": wk_ap, "Wv": wv_ap, "Wb": wb_ap, "Wq": wq_ap, "WoT": wo_ap, "bbeta": bb_ap, "y": y_ap}, TTL=TTL)
    return p.build(sd), ["x", "Win", "Wr", "Wk", "Wv", "Wb",
                                        "Wq", "WoT", "bbeta"]


def emit_l3_attn(p, c, aps, TTL=512):
    x_ap = aps["x"]
    wq_ap = aps["Wq"]
    wk_ap = aps["Wk"]
    wv_ap = aps["Wv"]
    wc_ap = aps["Wc"]
    idec_ap = aps["idecT"]
    mdec_ap = aps["mdec"]
    tmix_ap = aps["tmix"]
    y_ap = aps["y"]
    H = 16
    C = 128
    NC = S // C
    NT = S // TTL
    pool = [p.psum(f"pp{i}") for i in range(6)]
    ps_pcs = p.psum("ps_pcs", (32, 512))

    wbuf = [p.sbuf(f"wbuf_{k}", [128, D], BF16) for k in range(NCH)]
    wcs = p.sbuf("wcs", [1, D], BF16)

    def load_w(ap, colsum):
        for k in range(NCH):
            p.dma("gpsimd", wbuf[k][:], ap[k * 128:(k + 1) * 128, :],
                  writes=[wbuf[k]])
        if colsum:
            for s_ in range(D // 512):
                c0, c1 = s_ * 512, (s_ + 1) * 512
                for k in range(NCH):
                    p.op("tensor", lambda t, k=k, c0=c0, c1=c1: t.matmul(
                        pool[3][0:1, 0:512], c["ones_col_bf"][:],
                        wbuf[k][:, c0:c1], start=(k == 0), stop=(k == NCH - 1)),
                        reads=[wbuf[k], c["ones_col_bf"]], writes=[pool[3]])
                p.op("vector", lambda v, c0=c0, c1=c1: v.tensor_copy(
                    wcs[0:1, c0:c1], pool[3][0:1, 0:512]),
                    reads=[pool[3]], writes=[wcs])

    v_tm = [p.sbuf(f"vtm_{i}", [128, 16 * 65], BF16) for i in range(NC)]
    ln_bufs = alloc_ln_bufs(p, tt_len=TTL, xs_width=S, xb=None)
    xs = ln_bufs["xs"]
    negrm = ln_bufs["negrm"]
    xt = [p.sbuf(f"xt_{k}", [128, TTL], F32) for k in range(NCH)]
    vals = [p.sbuf(f"vals_{k}", [128, S], BF16) for k in range(NCH)]
    qh = p.sbuf("qh", [64, S], BF16)
    kh = p.sbuf("kh", [64, S], BF16)
    wsm_q = [p.sbuf(f"wsmq_{k}", [128, 64], BF16) for k in range(NCH)]
    wsm_k = [p.sbuf(f"wsmk_{k}", [128, 64], BF16) for k in range(NCH)]
    csq = p.sbuf("csq", [1, 64], BF16)
    csk = p.sbuf("csk", [1, 64], BF16)
    ewt = [p.sbuf(f"ewt_{i}", [128, TTL], BF16) for i in range(4)]
    pt = p.sbuf("pt", [128, TTL], BF16)
    etb = [p.sbuf(f"etb_{i}", [128, C], BF16) for i in range(2)]
    qmask = [p.sbuf(f"qmask_{j}", [128, TTL], BF16) for j in range(4)]
    mask_lo = p.sbuf("mask_lo", [128, C], BF16)
    chunkpats = p.sbuf("chunkpats", [128, 32 * NC], BF16)
    pcs_sb = p.sbuf("pcs_sb", [32, S], F32)
    outs_T = p.sbuf("outs_T", [128, 33 * NC], F32)
    mgt = p.sbuf("mgt", [128, 32], F32)
    ehp = p.sbuf("ehp", [128, 1], F32)
    idec = p.sbuf("idec", [64, 16], F32)
    mrow = p.sbuf("mrow", [1, 192], F32)
    ztr = p.sbuf("ztr", [1, 2 * C], F32)
    c2r = p.sbuf("c2r", [1, S], BF16)
    twqr = p.sbuf("twqr", [1, S], F32)
    winr = p.sbuf("winr", [1, 64], F32)
    zq = p.sbuf("zq", [128, 4], F32)
    store_deps = []

    p.dma("sync", idec[:], idec_ap, writes=[idec])
    p.dma("sync", mrow[0:1, 64:80], mdec_ap, writes=[mrow])
    p.dma("sync", mrow[0:1, 0:64], tmix_ap, writes=[mrow])
    tw = mrow[0:1, 80:144]
    p.op("scalar", lambda s: s.activation(tw, mrow[0:1, 0:64], AF.Sigmoid),
         reads=[mrow], writes=[mrow])
    p.op("scalar", lambda s: s.activation(mrow[0:1, 144:160], mrow[0:1, 64:80],
                                          AF.Exp, scale=64.0),
         reads=[mrow], writes=[mrow])
    p.op("vector", lambda v: v.tensor_scalar(
        mrow[0:1, 144:160], mrow[0:1, 144:160], 2.0, 0.0, op0=ALU.min,
        op1=ALU.max), reads=[mrow], writes=[mrow])
    mega_row = mrow[0:1, 144:160]
    p.op("vector", lambda v: v.tensor_copy(
        twqr[0:1, :].rearrange("o (r q) -> o r q", q=64),
        tw.rearrange("o (a q) -> o a q", a=1).to_broadcast([1, 32, 64])),
        reads=[mrow], writes=[twqr])

    cr512 = xt[0]
    p.op("vector", lambda v: v.memset(cr512[:], 1.0), writes=[cr512])
    p.op("vector", lambda v: v.tensor_tensor_scan(
        cr512[:], cr512[:], cr512[:], 0.0, op0=ALU.mult, op1=ALU.add),
        reads=[cr512], writes=[cr512])
    p.op("vector", lambda v: v.tensor_scalar_add(cr512[:], cr512[:], -1.0),
         reads=[cr512], writes=[cr512])
    for j in range(4):
        p.op("vector", lambda v, j=j: v.tensor_scalar(
            qmask[j][:], cr512[:], float(-j * 128), None, op0=ALU.add),
            reads=[cr512], writes=[qmask[j]])
        p.op("vector", lambda v, j=j: v.tensor_tensor(
            qmask[j][:], qmask[j][:], c["rowid"][:].to_broadcast([128, TTL]),
            op=ALU.is_le), reads=[qmask[j], c["rowid"]], writes=[qmask[j]])
    p.op("vector", lambda v: v.tensor_scalar(
        mask_lo[:], cr512[:, 0:C], 64.0, None, op0=ALU.is_lt),
        reads=[cr512], writes=[mask_lo])
    p.op("vector", lambda v: v.tensor_scalar(
        zq[:, 0:1], c["rowid"][:], 64.0, None, op0=ALU.is_ge),
        reads=[c["rowid"]], writes=[zq])
    p.op("vector", lambda v: v.tensor_scalar(
        mask_lo[:], mask_lo[:], zq[:, 0:1], None, op0=ALU.mult),
        reads=[mask_lo, zq], writes=[mask_lo])
    for qt in range(NC):
        p.op("vector", lambda v, qt=qt: v.tensor_scalar(
            zq[:, 1:2], zq[:, 0:1], 1.0, float(2 * qt), op0=ALU.mult,
            op1=ALU.add), reads=[zq], writes=[zq])
        p.op("vector", lambda v, qt=qt: v.tensor_tensor(
            chunkpats[:, 32 * qt:32 * (qt + 1)], c["colramp"][:, 0:32],
            zq[:, 1:2].to_broadcast([128, 32]), op=ALU.is_equal),
            reads=[c["colramp"], zq], writes=[chunkpats])

    for tt in range(NT):
        g0 = tt * TTL
        for k in range(NCH):
            p.dma("sync", xt[k][:], x_ap[k * 128:(k + 1) * 128, g0:g0 + TTL],
                  writes=[xt[k]])
        emit_ln(p, c, xt, ln_bufs, pool[4], pool[5], tt_len=TTL, off=g0)

    load_w(wv_ap, True)
    for i in range(NC):
        t0 = i * C
        for vs in range(2):
            c0, c1 = vs * 512, (vs + 1) * 512
            pb = pool[vs]
            for k in range(NCH):
                p.op("tensor", lambda t, k=k, c0=c0, c1=c1, pb=pb, t0=t0:
                     t.matmul(pb[:, 0:512], xs[k][:, t0:t0 + C],
                              wbuf[k][:, c0:c1], start=(k == 0), stop=False),
                     reads=[xs[k], wbuf[k]], writes=[pb])
            p.op("tensor", lambda t, c0=c0, c1=c1, pb=pb, t0=t0: t.matmul(
                pb[:, 0:512], negrm[0:1, t0:t0 + C], wcs[0:1, c0:c1],
                start=False, stop=True), reads=[negrm, wcs], writes=[pb])
            p.op("vector", lambda v, i=i, vs=vs, pb=pb: v.tensor_copy(
                v_tm[i][:].rearrange("p (h e) -> p h e", e=65)
                [:, vs * 8:(vs + 1) * 8, 0:64],
                pb[:, 0:512].rearrange("p (h e) -> p h e", e=64)),
                reads=[pb], writes=[v_tm[i]])
    for i in range(NC):
        p.op("vector", lambda v, i=i: v.memset(
            v_tm[i][:].rearrange("p (h e) -> p h e", e=65)[:, :, 64:65], 1.0),
            writes=[v_tm[i]])

    for h in range(H):
        for k in range(NCH):
            p.dma("gpsimd", wsm_q[k][:],
                  wq_ap[k * 128:(k + 1) * 128, h * 64:(h + 1) * 64],
                  writes=[wsm_q[k]])
            p.dma("gpsimd", wsm_k[k][:],
                  wk_ap[k * 128:(k + 1) * 128, h * 64:(h + 1) * 64],
                  writes=[wsm_k[k]])
        for (wsm, cs) in ((wsm_q, csq), (wsm_k, csk)):
            for k in range(NCH):
                p.op("tensor", lambda t, k=k, wsm=wsm: t.matmul(
                    pool[3][0:1, 0:64], c["ones_col_bf"][:], wsm[k][:],
                    start=(k == 0), stop=(k == NCH - 1)),
                    reads=[wsm[k], c["ones_col_bf"]], writes=[pool[3]])
            p.op("vector", lambda v, cs=cs: v.tensor_copy(
                cs[0:1, :], pool[3][0:1, 0:64]), reads=[pool[3]], writes=[cs])
        for tt in range(NT):
            t0 = tt * TTL
            for (wsm, cs, dst) in ((wsm_q, csq, qh), (wsm_k, csk, kh)):
                pb = pool[4]
                for k in range(NCH):
                    p.op("tensor", lambda t, k=k, wsm=wsm, t0=t0, pb=pb:
                         t.matmul(pb[0:64, 0:TTL], wsm[k][:],
                                  xs[k][:, t0:t0 + TTL], start=(k == 0),
                                  stop=False),
                         reads=[xs[k], wsm[k]], writes=[pb])
                p.op("tensor", lambda t, cs=cs, t0=t0, pb=pb: t.matmul(
                    pb[0:64, 0:TTL], cs[0:1, :], negrm[0:1, t0:t0 + TTL],
                    start=False, stop=True), reads=[cs, negrm], writes=[pb])
                p.op("vector", lambda v, t0=t0, pb=pb, dst=dst: v.tensor_copy(
                    dst[0:64, t0:t0 + TTL], pb[0:64, 0:TTL]),
                    reads=[pb], writes=[dst])
        p.op("scalar", lambda s, h=h: s.activation(
            ehp[0:64, 0:1], idec[:, h:h + 1], AF.Exp),
            reads=[idec], writes=[ehp])
        p.op("vector", lambda v: v.tensor_copy(ehp[64:128, 0:1],
                                               ehp[0:64, 0:1]),
             reads=[ehp], writes=[ehp])

        p.op("vector", lambda v: v.memset(pcs_sb[0:32, :], 0.0),
             writes=[pcs_sb])
        for qt in range(NC):
            q0 = qt * C
            nst = qt // 4 + 1
            for st in range(nst):
                s0 = st * 512
                sw = min(512, (qt + 1) * C - s0)
                pb = pool[st % 2]
                p.op("tensor", lambda t, q0=q0, s0=s0, sw=sw, pb=pb: t.matmul(
                    pb[:, 0:sw], qh[0:64, q0:q0 + C], kh[0:64, s0:s0 + sw],
                    start=True, stop=True), reads=[qh, kh], writes=[pb])
                p.op("scalar", lambda s, pb=pb, sw=sw, st=st: s.activation(
                    ewt[st][:, 0:sw], pb[:, 0:sw], AF.Exp, scale=0.125,
                    accum_out=zq[:, 0:1]), reads=[pb], writes=[ewt[st], zq])
                if st == nst - 1:
                    j = qt % 4
                    p.op("vector", lambda v, j=j, sw=sw, st=st: v.tensor_tensor(
                        ewt[st][:, 0:sw], ewt[st][:, 0:sw], qmask[j][:, 0:sw],
                        op=ALU.mult), reads=[ewt[st], qmask[j]],
                        writes=[ewt[st]])
                    p.op("vector", lambda v, sw=sw, st=st: v.reduce_sum(
                        zq[:, 0:1], ewt[st][:, 0:sw],
                        axis=mybir.AxisListType.X),
                        reads=[ewt[st]], writes=[zq])
                if st == 0:
                    p.op("vector", lambda v: v.tensor_copy(zq[:, 1:2],
                                                           zq[:, 0:1]),
                         reads=[zq], writes=[zq])
                else:
                    p.op("vector", lambda v: v.tensor_tensor(
                        zq[:, 1:2], zq[:, 1:2], zq[:, 0:1], op=ALU.add),
                        reads=[zq], writes=[zq])
            p.op("vector", lambda v: v.reciprocal(zq[:, 2:3], zq[:, 1:2]),
                 reads=[zq], writes=[zq])
            p.op("vector", lambda v: v.tensor_tensor(
                zq[:, 3:4], zq[:, 2:3], ehp[:, 0:1], op=ALU.mult),
                reads=[zq, ehp], writes=[zq])
            for st in range(nst):
                s0 = st * 512
                sw = min(512, (qt + 1) * C - s0)
                p.op("vector", lambda v, st=st, sw=sw: v.tensor_scalar(
                    ewt[st][:, 0:sw], ewt[st][:, 0:sw], zq[:, 3:4], None,
                    op0=ALU.mult), reads=[ewt[st], zq], writes=[ewt[st]])
                p.op("tensor", lambda t, st=st, sw=sw, qt=qt: t.matmul(
                    ps_pcs[0:32, 0:sw], chunkpats[:, 32 * qt:32 * (qt + 1)],
                    ewt[st][:, 0:sw], start=True, stop=True),
                    reads=[chunkpats, ewt[st]], writes=[ps_pcs])
                p.op("vector", lambda v, s0=s0, sw=sw: v.tensor_tensor(
                    pcs_sb[0:32, s0:s0 + sw], pcs_sb[0:32, s0:s0 + sw],
                    ps_pcs[0:32, 0:sw], op=ALU.add),
                    reads=[ps_pcs, pcs_sb], writes=[pcs_sb])


        p.op("vector", lambda v, h=h: v.tensor_copy(
            mrow[0:1, 160:192], mega_row[0:1, h:h + 1].to_broadcast([1, 32])),
            reads=[mrow], writes=[mrow])
        p.op("tensor", lambda t: t.matmul(
            pool[2][:, 0:32], c["ones_full_f"][0:1, :], mrow[0:1, 160:192],
            start=True, stop=True), reads=[mrow, c["ones_full_f"]],
            writes=[pool[2]])
        p.op("vector", lambda v: v.tensor_copy(mgt[:], pool[2][:, 0:32]),
             reads=[pool[2]], writes=[mgt])
        p.op("vector", lambda v: v.memset(
            outs_T[:].rearrange("p (i e) -> p i e", e=33)[:, :, 0:1], 0.0),
            writes=[outs_T])
        for i in range(NC):
            p.op("tensor", lambda t, i=i: t.transpose(
                pool[2][0:128, 0:32], pcs_sb[0:32, i * C:(i + 1) * C],
                c["ident_f"][0:32, 0:32]), reads=[pcs_sb, c["ident_f"]],
                writes=[pool[2]])
            p.op("vector", lambda v, i=i: v.tensor_tensor_scan(
                outs_T[:, 33 * i + 1:33 * (i + 1)], mgt[:],
                pool[2][0:128, 0:32], 0.0,
                op0=ALU.mult, op1=ALU.add), reads=[mgt, pool[2]],
                writes=[outs_T])
        for i in range(NC):
            p.op("tensor", lambda t, i=i: t.matmul(
                pool[2][0:1, 0:32], c["ones_full_f"][:, 0:1],
                outs_T[:, 33 * i:33 * i + 32], start=(i == 0),
                stop=(i == NC - 1)), reads=[outs_T, c["ones_full_f"]],
                writes=[pool[2]])
        p.op("vector", lambda v: v.tensor_scalar_add(
            winr[0:1, 0:32], pool[2][0:1, 0:32], 1e-5),
            reads=[pool[2]], writes=[winr])
        p.op("vector", lambda v: v.reciprocal(winr[0:1, 32:64],
                                              winr[0:1, 0:32]),
             reads=[winr], writes=[winr])
        p.op("vector", lambda v: v.tensor_scalar(
            c2r[0:1, :], twqr[0:1, :], -1.0, 1.0, op0=ALU.mult, op1=ALU.add),
            reads=[twqr], writes=[c2r])
        p.op("vector", lambda v: v.tensor_tensor(
            c2r[0:1, :].rearrange("o (r q) -> o r q", q=64),
            c2r[0:1, :].rearrange("o (r q) -> o r q", q=64),
            winr[0:1, 32:64].rearrange("o (r q) -> o r q", q=1)
            .to_broadcast([1, 32, 64]), op=ALU.mult),
            reads=[c2r, winr], writes=[c2r])

        for m in range(NC):
            q0 = m * C
            hv = h * 65
            if m > 0:
                p.op("tensor", lambda t, m=m, q0=q0: t.matmul(
                    pool[0][:, 0:C], kh[0:64, (m - 1) * C:m * C],
                    qh[0:64, q0:q0 + C], start=True, stop=True),
                    reads=[kh, qh], writes=[pool[0]])
                p.op("scalar", lambda s: s.activation(
                    etb[0][:], pool[0][:, 0:C], AF.Exp, scale=0.125),
                    reads=[pool[0]], writes=[etb[0]])
                p.op("vector", lambda v: v.tensor_tensor(
                    etb[0][:], etb[0][:], mask_lo[:], op=ALU.mult),
                    reads=[etb[0], mask_lo], writes=[etb[0]])
            p.op("tensor", lambda t, m=m, q0=q0: t.matmul(
                pool[1][:, 0:C], kh[0:64, m * C:(m + 1) * C],
                qh[0:64, q0:q0 + C], start=True, stop=True),
                reads=[kh, qh], writes=[pool[1]])
            p.op("scalar", lambda s: s.activation(
                etb[1][:], pool[1][:, 0:C], AF.Exp, scale=0.125),
                reads=[pool[1]], writes=[etb[1]])
            p.op("vector", lambda v: v.tensor_tensor(
                etb[1][:], etb[1][:], c["mask_ut"][:], op=ALU.mult),
                reads=[etb[1], c["mask_ut"]], writes=[etb[1]])
            first = True
            if m > 0:
                p.op("tensor", lambda t, m=m, hv=hv: t.matmul(
                    pool[2][0:64, 0:C], v_tm[m - 1][:, hv:hv + 64],
                    etb[0][:], start=True, stop=False),
                    reads=[v_tm[m - 1], etb[0]], writes=[pool[2]])
                p.op("tensor", lambda t, m=m, hv=hv: t.matmul(
                    pool[3][0:1, 0:C], v_tm[m - 1][:, hv + 64:hv + 65],
                    etb[0][:], start=True, stop=False),
                    reads=[v_tm[m - 1], etb[0]], writes=[pool[3]])
                first = False
            p.op("tensor", lambda t, m=m, hv=hv, first=first: t.matmul(
                pool[2][0:64, 0:C], v_tm[m][:, hv:hv + 64], etb[1][:],
                start=first, stop=True), reads=[v_tm[m], etb[1]],
                writes=[pool[2]])
            p.op("tensor", lambda t, m=m, hv=hv, first=first: t.matmul(
                pool[3][0:1, 0:C], v_tm[m][:, hv + 64:hv + 65], etb[1][:],
                start=first, stop=True), reads=[v_tm[m], etb[1]],
                writes=[pool[3]])
            p.op("vector", lambda v: v.tensor_copy(
                ztr[0:1, 0:C], pool[3][0:1, 0:C]),
                reads=[pool[3]], writes=[ztr])
            p.op("vector", lambda v: v.reciprocal(
                ztr[0:1, C:2 * C], ztr[0:1, 0:C]), reads=[ztr], writes=[ztr])
            p.op("vector", lambda v, q0=q0: v.tensor_tensor(
                ztr[0:1, C:2 * C], ztr[0:1, C:2 * C], twqr[0:1, q0:q0 + C],
                op=ALU.mult), reads=[ztr, twqr], writes=[ztr])
            p.op("tensor", lambda t: t.matmul(
                pool[4][0:64, 0:C], c["ones_full_f"][0:1, 0:64],
                ztr[0:1, C:2 * C], start=True, stop=True),
                reads=[ztr, c["ones_full_f"]], writes=[pool[4]])
            p.op("vector", lambda v: v.tensor_copy(
                etb[0][0:64, 0:C], pool[2][0:64, 0:C]),
                reads=[pool[2]], writes=[etb[0]])
            p.op("vector", lambda v, q0=q0, h=h: v.tensor_tensor(
                vals[h // 2][64 * (h % 2):64 * (h % 2) + 64, q0:q0 + C],
                etb[0][0:64, 0:C], pool[4][0:64, 0:C], op=ALU.mult),
                reads=[etb[0], pool[4]], writes=[vals[h // 2]])

        for qt in range(NT):
            q0 = qt * TTL
            nst = 4 * (qt + 1)
            for st in range(nst):
                s0 = st * C
                pb = pool[st % 2]
                p.op("tensor", lambda t, q0=q0, s0=s0, pb=pb: t.matmul(
                    pb[:, 0:TTL], kh[0:64, s0:s0 + C], qh[0:64, q0:q0 + TTL],
                    start=True, stop=True), reads=[kh, qh], writes=[pb])
                rs0 = qt * 8
                p.op("vector", lambda v, st=st, rs0=rs0, pb=pb: v.tensor_tensor(
                    pt[:].rearrange("p (r q) -> p r q", q=64),
                    pb[:, 0:TTL].rearrange("p (r q) -> p r q", q=64),
                    outs_T[:, 33 * st + rs0:33 * st + rs0 + 8]
                    .rearrange("p (r o) -> p r o", o=1)
                    .to_broadcast([128, 8, 64]), op=ALU.mult),
                    reads=[pb, outs_T], writes=[pt])
                p.op("tensor", lambda t, st=st, h=h: t.matmul(
                    pool[2][0:64, 0:TTL], v_tm[st][:, h * 65:h * 65 + 64],
                    pt[:], start=(st == 0), stop=(st == nst - 1)),
                    reads=[v_tm[st], pt], writes=[pool[2]])
            p.op("tensor", lambda t, q0=q0: t.matmul(
                pool[4][0:64, 0:TTL], c["ones_row_bf"][0:1, 0:64],
                c2r[0:1, q0:q0 + TTL], start=True, stop=True),
                reads=[c2r, c["ones_row_bf"]], writes=[pool[4]])
            hb = 64 * (h % 2)
            p.op("vector", lambda v, hb=hb: v.tensor_copy(
                pt[hb:hb + 64, 0:TTL], pool[2][0:64, 0:TTL]),
                reads=[pool[2]], writes=[pt])
            p.op("vector", lambda v, hb=hb: v.tensor_tensor(
                pt[hb:hb + 64, 0:TTL], pt[hb:hb + 64, 0:TTL],
                pool[4][0:64, 0:TTL],
                op=ALU.mult), reads=[pt, pool[4]], writes=[pt])
            p.op("vector", lambda v, q0=q0, h=h, hb=hb: v.tensor_tensor(
                vals[h // 2][hb:hb + 64, q0:q0 + TTL],
                vals[h // 2][hb:hb + 64, q0:q0 + TTL],
                pt[hb:hb + 64, 0:TTL], op=ALU.add),
                reads=[pt, vals[h // 2]], writes=[vals[h // 2]])

    load_w(wc_ap, False)
    for tt in range(NT):
        g0 = tt * TTL
        for o in range(NCH):
            pb = pool[o % 2]
            for k in range(NCH):
                p.op("tensor", lambda t, k=k, o=o, pb=pb, g0=g0: t.matmul(
                    pb[:, 0:TTL], wbuf[k][:, o * 128:(o + 1) * 128],
                    vals[k][:, g0:g0 + TTL], start=(k == 0),
                    stop=(k == NCH - 1)),
                    reads=[wbuf[k], vals[k]], writes=[pb])
            p.dma("sync", xt[o][:], x_ap[o * 128:(o + 1) * 128, g0:g0 + TTL],
                  writes=[xt[o]])
            p.op("vector", lambda v, o=o, pb=pb: v.tensor_tensor(
                xt[o][:], xt[o][:], pb[:, 0:TTL], op=ALU.add),
                reads=[pb, xt[o]], writes=[xt[o]])
            store_deps.append(p.dma(
                "scalar", y_ap[o * 128:(o + 1) * 128, g0:g0 + TTL],
                xt[o][:], reads=[xt[o]]))

    return store_deps[-NCH:]


def build_l3_attn(TTL=512):
    nc = bass.Bass("TRN2", target_bir_lowering=False, debug=False)
    p = Prog(nc)
    H = 16
    C = 128
    NC = S // C
    NT = S // TTL
    x_ap = nc.dram_tensor("x", [D, S], F32, kind="ExternalInput").ap()
    wq_ap = nc.dram_tensor("Wq", [D, D], F32, kind="ExternalInput").ap()
    wk_ap = nc.dram_tensor("Wk", [D, D], F32, kind="ExternalInput").ap()
    wv_ap = nc.dram_tensor("Wv", [D, D], F32, kind="ExternalInput").ap()
    wc_ap = nc.dram_tensor("Wc", [D, D], F32, kind="ExternalInput").ap()
    idec_ap = nc.dram_tensor("idecT", [64, 16], F32, kind="ExternalInput").ap()
    mdec_ap = nc.dram_tensor("mdec", [1, 16], F32, kind="ExternalInput").ap()
    tmix_ap = nc.dram_tensor("tmix", [1, 64], F32, kind="ExternalInput").ap()
    y_ap = nc.dram_tensor("y", [D, S], F32, kind="ExternalOutput").ap()

    c = emit_consts(p)
    sd = emit_l3_attn(p, c, {"x": x_ap, "Wq": wq_ap, "Wk": wk_ap, "Wv": wv_ap, "Wc": wc_ap, "idecT": idec_ap, "mdec": mdec_ap, "tmix": tmix_ap, "y": y_ap}, TTL=TTL)
    return p.build(sd), ["x", "Wq", "Wk", "Wv", "Wc",
                                        "idecT", "mdec", "tmix"]


MEGA_GROUPS = {
    "l0moe": {"Wg": [D, 8], "Ws1": [D, 2048], "Ws3": [D, 2048],
              "Ws2": [2048, D], "We1": [8, D, 2048], "We3": [8, D, 2048],
              "We2": [8, 2048, D]},
    "l0attn": {"Wq": [D, D], "Wk": [D, D], "Wv": [D, D], "Wo": [D, D],
               "bq": [1, D], "bk": [1, D], "bv": [1, D], "bo": [1, D]},
    "l1ffn": {"W1": [D, 4096], "W2": [4096, D], "b1": [4096], "b2": [D],
              "qph": [4096], "qwr": [4096], "qwi": [4096]},
    "l1attn": {"Win": [D, D], "Wr": [D, 4], "Wk": [D, 128], "Wv": [D, 128],
               "Wb": [D, 128], "Wq": [D, 32], "WoT": [32, D],
               "bbeta": [128, 1]},
    "l2ffn": {"Wgt": [2816, D], "Wup": [2816, D], "Wdn": [D, 2816]},
    "l2attn": {"Wqc": [16, D, 128], "Wkc": [16, D, 128], "Wv": [D, D],
               "Wo": [D, D]},
    "l3moe": {"GW": [D, 8], "GP": [8, D, 2048], "UP": [8, D, 2048],
              "DP": [8, 2048, D], "prevmask": [8, 1]},
    "l3attn": {"Wq": [D, D], "Wk": [D, D], "Wv": [D, D], "Wc": [D, D],
               "idecT": [64, 16], "mdec": [1, 16], "tmix": [1, 64]},
}

MEGA_SEQ = [("l0moe", emit_l0_moe, 512), ("l0attn", emit_l0_attn, 512),
            ("l0moe", emit_l0_moe, 512), ("l1ffn", emit_l1_ffn, 256),
            ("l1attn", emit_l1_attn, 512), ("l1ffn", emit_l1_ffn, 256),
            ("l2ffn", emit_l2_ffn, 256), ("l2attn", emit_l2_attn, 512),
            ("l2ffn", emit_l2_ffn, 256), ("l3moe", emit_l3_moe, 512),
            ("l3attn", emit_l3_attn, 512), ("l3moe", emit_l3_moe, 512)]


def build_mega():
    nc = bass.Bass("TRN2", target_bir_lowering=False, debug=False)
    p = Prog(nc)
    names = ["x"]
    gaps = {}
    for g, ins in MEGA_GROUPS.items():
        gaps[g] = {}
        for k, shp in ins.items():
            nm = f"{g}_{k}"
            names.append(nm)
            gaps[g][k] = nc.dram_tensor(nm, list(shp), F32,
                                        kind="ExternalInput").ap()
    x_ap = nc.dram_tensor("x", [D, S], F32, kind="ExternalInput").ap()
    y_ap = nc.dram_tensor("y", [D, S], F32, kind="ExternalOutput").ap()
    ping = nc.dram_tensor("xping", [D, S], F32, kind="Internal").ap()
    pong = nc.dram_tensor("xpong", [D, S], F32, kind="Internal").ap()

    c = emit_consts(p)
    cur = x_ap
    sd = None
    for i, (g, emitter, ttl) in enumerate(MEGA_SEQ):
        nxt = y_ap if i == len(MEGA_SEQ) - 1 else (
            ping if i % 2 == 0 else pong)
        p.scope_open(f"s{i:02d}")
        aps = dict(gaps[g])
        aps["x"] = cur
        aps["y"] = nxt
        sd = emitter(p, c, aps, TTL=ttl)
        p.scope_close()
        cur = nxt
    return p.build(sd), names



_BUILT = {}


def _get(key, builder, *args):
    if key not in _BUILT:
        _BUILT[key] = builder(*args)
    return _BUILT[key]


def run_launch(nc_names, per_core_inputs):
    nc, names = nc_names
    in_maps = []
    for ci in range(NCORES):
        m = {}
        for k in names:
            v = per_core_inputs[k]
            m[k] = v[ci] if isinstance(v, list) else np.asarray(v)
        in_maps.append(m)
    return run_bass_kernel_spmd(nc, in_maps, core_ids=list(range(NCORES)))


def _group_feeds(inp):
    def T(m):
        return np.ascontiguousarray(np.asarray(m).T)

    l0moe_feed = {
        "Wg": inp["L0_ffn_Wg"], "Ws1": inp["L0_ffn_Ws1"],
        "Ws3": inp["L0_ffn_Ws3"], "Ws2": inp["L0_ffn_Ws2"],
        "We1": inp["L0_ffn_We1"], "We3": inp["L0_ffn_We3"],
        "We2": inp["L0_ffn_We2"]}
    l0attn_feed = {
        "Wq": inp["L0_attn_Wq"], "Wk": inp["L0_attn_Wk"],
        "Wv": inp["L0_attn_Wv"], "Wo": inp["L0_attn_Wo"],
        "bq": inp["L0_attn_bq"][None, :], "bk": inp["L0_attn_bk"][None, :],
        "bv": inp["L0_attn_bv"][None, :], "bo": inp["L0_attn_bo"][None, :]}
    l1ffn_feed = {
        "W1": inp["L1_ffn_W1"], "W2": inp["L1_ffn_W2"],
        "b1": inp["L1_ffn_b1"], "b2": inp["L1_ffn_b2"],
        "qph": inp["L1_ffn_quantum_phases"],
        "qwr": inp["L1_ffn_quantum_weights_real"],
        "qwi": inp["L1_ffn_quantum_weights_imag"]}
    wkv = inp["L1_attn_W_kv"].reshape(4, 2, 32, D)
    l1attn_feed = {
        "Win": T(inp["L1_attn_W_in"]), "Wr": T(inp["L1_attn_W_router"]),
        "Wk": T(wkv[:, 0].reshape(128, D)), "Wv": T(wkv[:, 1].reshape(128, D)),
        "Wb": T(inp["L1_attn_W_beta"]), "Wq": T(inp["L1_attn_W_q"]),
        "WoT": T(inp["L1_attn_W_out"]),
        "bbeta": np.ascontiguousarray(
            inp["L1_attn_b_beta"].reshape(128, 1))}
    l2ffn_feed = {
        "Wgt": inp["L2_ffn_w_gate"], "Wup": inp["L2_ffn_w_up"],
        "Wdn": inp["L2_ffn_w_down"]}
    perm = np.arange(64) ^ 1
    l2attn_feed = {
        "Wqc": np.ascontiguousarray(np.concatenate(
            [inp["L2_attn_Wq"], inp["L2_attn_Wq"][:, :, perm]], -1)),
        "Wkc": np.ascontiguousarray(np.concatenate(
            [inp["L2_attn_Wk"], inp["L2_attn_Wk"][:, :, perm]], -1)),
        "Wv": np.ascontiguousarray(
            inp["L2_attn_Wv"].transpose(1, 0, 2).reshape(D, D)),
        "Wo": inp["L2_attn_Wo"]}
    pms = []
    for ci in range(NCORES):
        pm = np.zeros((8, 1), np.float32)
        pm[:ci] = 1.0
        pms.append(pm)
    l3moe_feed = {
        "GW": T(inp["L3_ffn_gate_w"]),
        "GP": np.ascontiguousarray(
            inp["L3_ffn_gate_proj_w"].transpose(0, 2, 1)),
        "UP": np.ascontiguousarray(
            inp["L3_ffn_up_proj_w"].transpose(0, 2, 1)),
        "DP": np.ascontiguousarray(
            inp["L3_ffn_down_proj_w"].transpose(0, 2, 1)),
        "prevmask": pms}
    l3attn_feed = {
        "Wq": T(inp["L3_attn_Wq"]), "Wk": T(inp["L3_attn_Wk"]),
        "Wv": T(inp["L3_attn_Wv"]), "Wc": T(inp["L3_attn_Wc"]),
        "idecT": T(inp["L3_attn_index_decays"]),
        "mdec": np.ascontiguousarray(inp["L3_attn_mega_decays"][None, :]),
        "tmix": np.ascontiguousarray(inp["L3_attn_terrace_mix"][None, :])}

    return {"l0moe": l0moe_feed, "l0attn": l0attn_feed, "l1ffn": l1ffn_feed,
            "l1attn": l1attn_feed, "l2ffn": l2ffn_feed,
            "l2attn": l2attn_feed, "l3moe": l3moe_feed,
            "l3attn": l3attn_feed}


def kernel(**inp):
    inp = {k: np.asarray(v) for k, v in inp.items()}
    grp = _group_feeds(inp)
    feeds = {}
    for g, d in grp.items():
        for k, v in d.items():
            feeds[f"{g}_{k}"] = v
    feeds["x"] = [np.ascontiguousarray(inp["L0_attn_x"][b].T)
                  for b in range(NCORES)]
    nc_names = _get("mega", build_mega)
    res = run_launch(nc_names, feeds)
    out = np.stack([np.asarray(res.results[ci]["y"]).T
                    for ci in range(NCORES)])
    return np.ascontiguousarray(out).astype(np.float32)


def kernel_multi(**inp):
    inp = {k: np.asarray(v) for k, v in inp.items()}
    grp = _group_feeds(inp)
    xcur = [np.ascontiguousarray(inp["L0_attn_x"][b].T)
            for b in range(NCORES)]

    def run(key, builder, feeds):
        nonlocal xcur
        nc_names = _get(key, builder)
        feeds = dict(feeds)
        feeds["x"] = xcur
        res = run_launch(nc_names, feeds)
        xcur = [np.asarray(res.results[ci]["y"]) for ci in range(NCORES)]

    run("l0moe", build_l0_moe, grp["l0moe"])
    run("l0attn", build_l0_attn, grp["l0attn"])
    run("l0moe", build_l0_moe, grp["l0moe"])
    run("l1ffn", build_l1_ffn, grp["l1ffn"])
    run("l1attn", build_l1_attn, grp["l1attn"])
    run("l1ffn", build_l1_ffn, grp["l1ffn"])
    run("l2ffn", build_l2_ffn, grp["l2ffn"])
    run("l2attn", build_l2_attn, grp["l2attn"])
    run("l2ffn", build_l2_ffn, grp["l2ffn"])
    run("l3moe", build_l3_moe, grp["l3moe"])
    run("l3attn", build_l3_attn, grp["l3attn"])
    run("l3moe", build_l3_moe, grp["l3moe"])

    out = np.stack([xcur[ci].T for ci in range(NCORES)])
    return np.ascontiguousarray(out).astype(np.float32)
